# version 14
# baseline (speedup 1.0000x reference)
import numpy as np
import ml_dtypes
import concourse.bass as bass
import concourse.mybir as mybir
from concourse.bass_utils import run_bass_kernel_spmd

F32 = mybir.dt.float32
BF16 = mybir.dt.bfloat16
I32 = mybir.dt.int32
AF = mybir.ActivationFunctionType
ALU = mybir.AluOpType
AXX = mybir.AxisListType.X

NCORES = 8
S = 16
B = 256
D = 1024
LN2C = float(np.log(2.0))
EPS = 1e-5
f32 = np.float32


def build_program():
    nc = bass.Bass("TRN2", target_bir_lowering=False, debug=False,
                   detect_race_conditions=False)

    x_d = nc.dram_tensor("x", [S, B, D], F32, kind="ExternalInput")
    w_d = nc.dram_tensor("w_attn", [D, 4 * D], F32, kind="ExternalInput")
    u_d = nc.dram_tensor("u_attn", [D, 4 * D], F32, kind="ExternalInput")
    bias_d = nc.dram_tensor("bias_attn", [4 * D], F32, kind="ExternalInput")
    wmod_d = nc.dram_tensor("wmod", [D, 6 * D], F32, kind="ExternalInput")
    cmod_d = nc.dram_tensor("cmod", [S, D], F32, kind="ExternalInput")
    wffn_d = nc.dram_tensor("wffn", [D, D], F32, kind="ExternalInput")
    bffn_d = nc.dram_tensor("bffn", [1, D], F32, kind="ExternalInput")
    dw_d = nc.dram_tensor("dw", [D, 32], F32, kind="ExternalInput")
    dact_d = nc.dram_tensor("dact", [1, 1], F32, kind="ExternalInput")
    identb_d = nc.dram_tensor("identb", [128, 128], BF16, kind="ExternalInput")
    identf_d = nc.dram_tensor("identf", [128, 128], F32, kind="ExternalInput")
    ones_d = nc.dram_tensor("ones1", [1, 128], BF16, kind="ExternalInput")
    xnew_d = nc.dram_tensor("xnew_h", [S, B, D], F32, kind="Internal")
    out_d = nc.dram_tensor("out", [S, B, D], F32, kind="ExternalOutput")

    arena_w = nc.alloc_sbuf_tensor("arena_w", [128, 16 * 4096], BF16)

    def wu(k):
        return arena_w[:, 4096 * k:4096 * (k + 1)]

    def wmod_t(kk):
        off = 4096 * 4 + 6144 * kk
        return arena_w[:, off:off + 6144]

    AB = 17408
    arena_B = nc.alloc_sbuf_tensor("arena_B", [128, AB], BF16)

    def _b(off, n):
        return arena_B[:, off:off + n]
    gst_o = 0
    hnT_o = [4096, 6144]
    hT_o = [8192, 10240]
    ctt_o = 12288
    hs_o = 12800
    xln_o = 14848
    wqT_o = 0
    wf_o = [8192, 9216]
    xq_o = [10240, 11264]
    xqT_o = [12288, 13312]
    tA_o, tB_o, tC_o = 14336, 15360, 16384
    tD_o = wf_o[0]
    tD2_o = xq_o[1]
    csil_o = 0

    AC = 8192
    arena_C = nc.alloc_sbuf_tensor("arena_C", [128, AC], F32)

    def _c(off, n):
        return arena_C[:, off:off + n]
    cT_o = 0
    xin_o = [2048, 4096, 6144]
    xr_o = [0, 1024]
    tln_o = 2048
    h2_o = 3072
    outst_o = [4096, 5120]
    bgb_o = 6144
    xr4_o = [0, 1024, 4096, 5120]
    cin_o = 0
    modn_o = 1024

    cbS = arena_w[:, 0:1024]
    sbS = arena_w[:, 1024:2048]
    identb = nc.alloc_sbuf_tensor("identb_s", [128, 128], BF16)
    identf = nc.alloc_sbuf_tensor("identf_s", [128, 128], F32)
    ones1 = nc.alloc_sbuf_tensor("ones1_s", [1, 128], BF16)
    siluCT = nc.alloc_sbuf_tensor("siluCT", [128, 128], BF16)
    modT = nc.alloc_sbuf_tensor("modT", [128, 384], F32)
    modnatb = nc.alloc_sbuf_tensor("modnatb", [16, 3072], BF16)
    biasT = nc.alloc_sbuf_tensor("biasT", [128, 32], F32)
    fbias = nc.alloc_sbuf_tensor("fbias", [1, 1024], BF16)
    dwh = nc.alloc_sbuf_tensor("dwh", [128, 256], BF16)
    dhalf = nc.alloc_sbuf_tensor("dhalf", [128, 1], F32)
    dactb = nc.alloc_sbuf_tensor("dactb", [1, 1], BF16)
    smalls = nc.alloc_sbuf_tensor("smalls", [128, 24], F32)
    fsm = nc.alloc_sbuf_tensor("fsm", [128, 448], F32)
    fti = nc.alloc_sbuf_tensor("fti", [128, 64], I32)
    scb = nc.alloc_sbuf_tensor("scb", [128, 64], BF16)

    tmpm = arena_C[:, 7168:7296]
    tmpm2 = arena_C[:, 7296:7424]
    dwt = arena_C[:, 7424:7680]
    biasN = arena_C[:32, 7680:7808]

    C_S1, C_S2, C_MEAN, C_MS, C_MSQ, C_VAR, C_RINV, C_NM = 0, 2, 4, 6, 8, 10, 12, 14
    C_DACT, C_T4D = 16, 17

    def sm(col):
        return smalls[:, col:col + 1]

    F_AMAX, F_AMX, F_L, F_U, F_TF, F_CM, F_EB = 0, 32, 64, 96, 128, 160, 192
    F_RS = F_U

    def fs(c0, par=0):
        c = 224 * par + c0
        return fsm[:, c:c + 32]

    def ftip(par):
        return fti[:, 32 * par:32 * (par + 1)]

    def scbp(par):
        return scb[:, 32 * par:32 * (par + 1)]

    pf = nc.alloc_psum_tensor("pf", [128, 2048], F32)
    pf2 = nc.alloc_psum_tensor("pf2", [128, 1024], F32)
    pt = nc.alloc_psum_tensor("pt", [128, 2048], BF16)

    def gslot(p):
        i = p % 8
        c = 512 * (i % 4) + 256 * (i // 4)
        return pf[:, c:c + 256]

    sems = {}
    for name in ["pd", "wmod", "ww", "wu2", "wsm", "pr", "pr2", "xin00", "xin01", "xin10", "xin11",
                 "sq", "stat", "rinv", "nm",
                 "ln", "tphn", "hn", "gmm", "gev", "cupd", "tanh", "cv", "hs",
                 "tphs", "xn", "xst0", "xst1",
                 "wqd0", "wqd1", "wqa", "wql", "wqu", "wqi", "wqx", "wqc", "wq", "wqt",
                 "wqe", "rb0", "rb1", "fsq", "fst", "fri", "fnm", "fno", "fbc",
                 "fbgbe", "fmo", "fam", "fln", "fu", "fif", "fex", "fsc",
                 "fq", "ftp", "fxq", "fmm", "fout", "ost0", "ost1", "vt", "st2",
                 "fbe", "hn1", "gq",
                 "xi00", "xi01", "xi10", "xi11", "xi20", "xi21"]:
        sems[name] = nc.alloc_semaphore("s_" + name)

    def SW(eng, name, val):
        if val > 0:
            eng.wait_ge(sems[name], int(val))

    wu_waited = set()

    def wu_wait(eng, k):
        key = "ww" if k < 8 else "wu2"
        if key not in wu_waited:
            wu_waited.add(key)
            SW(eng, key, 128)

    def rep32(ap):
        return ap.rearrange("p (b o) -> p b o", o=1).broadcast_to((128, 32, 32))

    def v3(ap):
        return ap.rearrange("p (b i) -> p b i", i=32)

    def stair_ops(eng, src, xq_ap, tA, tB, tC, tD, par, b0, b1, use_stt=True):
        nb = b1 - b0
        c0, c1 = 32 * b0, 32 * b1

        def v3c(ap):
            return ap[:, c0:c1].rearrange("p (b i) -> p b i", i=32)

        def repc(ap):
            return ap[:, b0:b1].rearrange("p (b o) -> p b o", o=1
                                          ).broadcast_to((128, nb, 32))

        def sl(ap):
            return ap[:, c0:c1]

        def stt(out, in0, sc, in1, op0, op1):
            if use_stt:
                eng.scalar_tensor_tensor(out, in0, sc, in1, op0, op1)
            else:
                eng.tensor_scalar(out, in0, sc, None, op0)
                eng.tensor_tensor(out, out, in1, op1)

        eng.tensor_tensor(v3c(tA), v3c(src), repc(fs(F_RS, par)), ALU.mult)
        stt(sl(tB), sl(tA), -1.0, sl(tA), ALU.mult, ALU.max)
        eng.tensor_scalar(sl(tC), sl(tB), 0.25, None, ALU.is_ge)
        eng.tensor_scalar(sl(tD), sl(tB), 0.75, None, ALU.is_ge)
        eng.tensor_add(sl(tC), sl(tC), sl(tD))
        eng.tensor_scalar(sl(tD), sl(tB), 1.25, None, ALU.is_ge)
        eng.tensor_add(sl(tC), sl(tC), sl(tD))
        eng.tensor_scalar(sl(tD), sl(tB), 1.75, None, ALU.is_ge)
        eng.tensor_add(sl(tC), sl(tC), sl(tD))
        eng.tensor_scalar(sl(tD), sl(tB), 2.5, None, ALU.is_ge)
        eng.tensor_scalar(sl(tB), sl(tB), 3.5, None, ALU.is_ge)
        eng.tensor_add(sl(tD), sl(tD), sl(tB))
        stt(sl(tC), sl(tC), 0.5, sl(tD), ALU.mult, ALU.add)
        stt(sl(tD), sl(tA), -1.0, sl(tA), ALU.mult, ALU.max)
        eng.tensor_scalar(sl(tD), sl(tD), 5.0, None, ALU.is_ge)
        stt(sl(tC), sl(tD), 2.0, sl(tC), ALU.mult, ALU.add)
        eng.tensor_scalar(sl(tD), sl(tA), 0.0, 2.0, ALU.is_ge, ALU.mult)
        stt(sl(tD), sl(tD), -1.0, sl(tC), ALU.add, ALU.mult)
        eng.tensor_tensor(v3c(xq_ap), v3c(tD), repc(scbp(par)), ALU.mult)

    def stair_ops_pool(eng, src, xq_ap, tA, tB, tC, tD, par, b0, b1):
        nb = b1 - b0
        c0, c1 = 32 * b0, 32 * b1

        def v3c(ap):
            return ap[:, c0:c1].rearrange("p (b i) -> p b i", i=32)

        def repc(ap):
            return ap[:, b0:b1].rearrange("p (b o) -> p b o", o=1
                                          ).broadcast_to((128, nb, 32))

        def sl(ap):
            return ap[:, c0:c1]

        eng.tensor_tensor(v3c(tA), v3c(src), repc(fs(F_RS, par)), ALU.mult)
        eng.tensor_tensor(sl(tB), sl(tA), sl(tA), ALU.mult)
        eng.tensor_scalar(sl(tC), sl(tB), 0.0625, None, ALU.is_ge)
        eng.tensor_scalar(sl(tD), sl(tB), 0.5625, None, ALU.is_ge)
        eng.tensor_add(sl(tC), sl(tC), sl(tD))
        eng.tensor_scalar(sl(tD), sl(tB), 1.5625, None, ALU.is_ge)
        eng.tensor_add(sl(tC), sl(tC), sl(tD))
        eng.tensor_scalar(sl(tD), sl(tB), 3.0625, None, ALU.is_ge)
        eng.tensor_add(sl(tC), sl(tC), sl(tD))
        eng.tensor_scalar(sl(tC), sl(tC), 0.5, None, ALU.mult)
        eng.tensor_scalar(sl(tD), sl(tB), 6.25, None, ALU.is_ge)
        eng.tensor_add(sl(tC), sl(tC), sl(tD))
        eng.tensor_scalar(sl(tD), sl(tB), 12.25, None, ALU.is_ge)
        eng.tensor_add(sl(tC), sl(tC), sl(tD))
        eng.tensor_scalar(sl(tD), sl(tB), 25.0, 2.0, ALU.is_ge, ALU.mult)
        eng.tensor_add(sl(tC), sl(tC), sl(tD))
        eng.tensor_scalar(sl(tD), sl(tA), 0.0, 2.0, ALU.is_ge, ALU.mult)
        eng.tensor_scalar(sl(tD), sl(tD), -1.0, None, ALU.add)
        eng.tensor_tensor(sl(tD), sl(tD), sl(tC), ALU.mult)
        eng.tensor_tensor(v3c(xq_ap), v3c(tD), repc(scbp(par)), ALU.mult)

    with nc.Block() as blk:
        @blk.sync
        def _(sync):
            sync.dma_start(out=identb[:, :], in_=identb_d[:, :]).then_inc(sems["pd"], 16)
            sync.dma_start(out=identf[:, :], in_=identf_d[:, :]).then_inc(sems["pd"], 16)
            sync.dma_start(out=ones1[:, :], in_=ones_d[:, :]).then_inc(sems["pd"], 16)
            sync.dma_start(out=_c(cin_o, 1024)[:16, :], in_=cmod_d[:, :]).then_inc(sems["pd"], 16)
            sync.dma_start(out=biasN[:, :],
                           in_=bias_d[:].rearrange("(m p) -> m p", p=128)).then_inc(sems["pd"], 16)
            sync.dma_start(out=dwt[:, :].rearrange("p (a c) -> p a c", c=32),
                           in_=dw_d[:, :].rearrange("(a p) c -> p a c", p=128)).then_inc(sems["pd"], 16)

        @blk.gpsimd
        def _(g):
            for kk in range(8):
                g.dma_start(out=wmod_t(kk), in_=wmod_d[128 * kk:128 * (kk + 1), :]).then_inc(sems["wmod"], 16)
            for k in range(4):
                g.dma_start(out=wu(k), in_=w_d[128 * k:128 * (k + 1), :]).then_inc(sems["ww"], 16)
            g.dma_start(out=fbias[:, :], in_=bffn_d[:, :]).then_inc(sems["wsm"], 16)
            g.dma_start(out=dactb[:, :], in_=dact_d[:, :]).then_inc(sems["wsm"], 16)

    with nc.Block() as blk:
        @blk.scalar
        def _(s):
            SW(s, "pd", 96)
            s.activation(_b(csil_o, 1024)[:16, :], _c(cin_o, 1024)[:16, :],
                         AF.Silu).then_inc(sems["pr"], 1)
            SW(s, "pd", 96)
            s.activation(dwh[:, :], dwt[:, :], AF.Tanh, scale=4.0).then_inc(sems["pr"], 1)

        @blk.tensor
        def _(t):
            SW(t, "pr", 1)
            SW(t, "pd", 96)
            for kk in range(8):
                t.transpose(pt[:, 128 * kk:128 * kk + 16],
                            _b(csil_o, 1024)[:16, 128 * kk:128 * (kk + 1)],
                            identb[:16, :16]).then_inc(sems["pr2"], 1)
            SW(t, "pd", 96)
            t.transpose(pf[:, 0:32], biasN[:, :], identf[:32, :32]).then_inc(sems["pr2"], 1)
            SW(t, "wsm", 32)
            t.matmul(pf[:, 32:33], ones1[:, :], dactb[:, :],
                     start=True, stop=True).then_inc(sems["pr2"], 1)

        @blk.vector
        def _(v):
            SW(v, "pr2", 8)
            v.tensor_copy(
                siluCT[:, :].rearrange("p (j c) -> p j c", c=16),
                pt[:, :1024].rearrange("p (j c) -> p j c", c=128)[:, :, :16])
            SW(v, "pr2", 10)
            v.tensor_copy(biasT[:, :], pf[:, 0:32])
            v.tensor_copy(sm(C_DACT), pf[:, 32:33]).then_inc(sems["pr"], 1)
            v.tensor_scalar(dwh[:, :], dwh[:, :], 0.5, None, ALU.mult)
            SW(v, "pr", 4)
            v.tensor_scalar(dhalf[:, :], sm(C_T4D), 0.5, None, ALU.mult)

        @blk.scalar
        def _(s2_):
            SW(s2_, "pr", 3)
            s2_.activation(sm(C_T4D), sm(C_DACT), AF.Tanh,
                           scale=4.0).then_inc(sems["pr"], 1)


    with nc.Block(no_gpsimd_drain=True) as blk:
        @blk.tensor
        def _(t):
            SW(t, "wmod", 128)
            for c in range(12):
                reg = pf[:16, 512 * (c % 2):512 * (c % 2) + 512]
                if c >= 2:
                    SW(t, "pr", c + 3)
                for kk in range(8):
                    mm = t.matmul(reg, siluCT[:, 16 * kk:16 * (kk + 1)],
                                  wmod_t(kk)[:, 512 * c:512 * (c + 1)],
                                  start=(kk == 0), stop=(kk == 7))
                    if kk == 7:
                        mm.then_inc(sems["pr2"], 1)

        @blk.scalar
        def _(s):
            for c in range(12):
                SW(s, "pr2", 11 + c)
                s.copy(_c(modn_o, 6144)[:16, 512 * c:512 * (c + 1)],
                       pf[:16, 512 * (c % 2):512 * (c % 2) + 512]).then_inc(sems["pr"], 1)

    with nc.Block(no_gpsimd_drain=True) as blk:
        @blk.tensor
        def _(t):
            SW(t, "pr", 16)
            for jj in range(24):
                tp = t.transpose(pf[:, 1024 + 16 * jj:1024 + 16 * (jj + 1)],
                                 _c(modn_o, 6144)[:16, 128 * jj:128 * (jj + 1)],
                                 identf[:16, :16])
                if jj == 23:
                    tp.then_inc(sems["pr2"], 1)

        @blk.vector
        def _(v):
            SW(v, "pr2", 23)
            v.tensor_copy(
                modT[:, :].rearrange("p (j c) -> p j c", c=16),
                pf[:, 1024:1024 + 384].rearrange("p (j c) -> p j c", c=16))
            v.tensor_scalar(modT[:, 128:256], modT[:, 128:256], 1.0, None, ALU.add)
            ga = modT[:, 256:384]
            v.tensor_scalar(tmpm[:, :], ga, 0.0, None, ALU.is_ge)
            v.tensor_scalar(tmpm2[:, :], ga, 0.25, None, ALU.max)
            v.tensor_scalar(ga, ga, -0.25, None, ALU.min).then_inc(sems["vt"], 1)
            v.wait_ge(sems["vt"], 1)
            v.tensor_sub(tmpm2[:, :], tmpm2[:, :], ga).then_inc(sems["vt"], 1)
            v.wait_ge(sems["vt"], 2)
            v.tensor_mul(tmpm[:, :], tmpm[:, :], tmpm2[:, :]).then_inc(sems["vt"], 1)
            v.wait_ge(sems["vt"], 3)
            v.tensor_add(ga, ga, tmpm[:, :])
            nat = _c(modn_o, 6144)[:16, :]
            v.tensor_scalar(nat[:, 4096:5120], nat[:, 4096:5120], 1.0, None, ALU.add)
            gb = nat[:, 5120:6144]
            m16 = nat[:, 0:1024]
            a16 = nat[:, 1024:2048]
            v.tensor_scalar(m16, gb, 0.0, None, ALU.is_ge)
            v.tensor_scalar(a16, gb, 0.25, None, ALU.max)
            v.tensor_scalar(gb, gb, -0.25, None, ALU.min).then_inc(sems["vt"], 1)
            v.wait_ge(sems["vt"], 4)
            v.tensor_sub(a16, a16, gb).then_inc(sems["vt"], 1)
            v.wait_ge(sems["vt"], 5)
            v.tensor_mul(m16, m16, a16).then_inc(sems["vt"], 1)
            v.wait_ge(sems["vt"], 6)
            v.tensor_add(gb, gb, m16).then_inc(sems["vt"], 1)
            v.wait_ge(sems["vt"], 7)
            v.tensor_copy(modnatb[:, :], nat[:, 3072:6144])

    def gst_sl(role, j):
        return _b(gst_o + (role * 2 + (j % 2)) * 256, 256)

    def hnT(par):
        return _b(hnT_o[par], 2048)

    def hT(par):
        return _b(hT_o[par], 2048)

    with nc.Block() as blk:
        @blk.gpsimd
        def _(g):
            for k in range(4, 8):
                g.dma_start(out=wu(k), in_=w_d[128 * k:128 * (k + 1), :]).then_inc(sems["ww"], 16)
            for k in range(8):
                g.dma_start(out=wu(8 + k), in_=u_d[128 * k:128 * (k + 1), :]).then_inc(sems["wu2"], 16)
            SW(g, "gmm", 32 * S)
            for oi in range(8):
                SW(g, "wq", oi - 1)
                g.dma_start(out=_b(wf_o[oi % 2], 1024),
                            in_=wffn_d[128 * oi:128 * (oi + 1), :]).then_inc(sems["wqd%d" % (oi % 2)], 16)
            for k in range(1, 2 * S + 1):
                j = k - 1
                par = j % 2
                SW(g, "fmo", k)
                SW(g, "fsc", j + 1)
                SW(g, "ftp", 8 * (j - 1))
                if j == 0:
                    SW(g, "wqt", 64)
                stair_ops_pool(g, _c(h2_o, 1024), _b(xq_o[par], 1024), _b(tA_o, 1024),
                               _b(tB_o, 1024), _b(tC_o, 1024), _b(tD_o, 1024), par,
                               20, 32)
                g.engine_nop().then_inc(sems["gq"], 1)

        @blk.tensor
        def _(t):
            def hn_burst(u, h):
                SW(t, "ln", 2 * u + h + 1)
                if h == 0:
                    SW(t, "hn", 8 * u)
                else:
                    SW(t, "xn", 2 * u - 2)
                    SW(t, "hn1", 8 * u)
                base = 0 if h == 0 else 1024
                for j in range(8):
                    t.transpose(pt[:, base + 128 * j:base + 128 * (j + 1)],
                                _b(xln_o + 1024 * h, 1024)[:, 128 * j:128 * (j + 1)],
                                identb[:, :]).then_inc(sems["tphn"], 1)

            def hs_burst(u, h):
                SW(t, "xn", 2 * u + h)
                SW(t, "hn1", min(8 * (u + 1) + 8, 8 * S))
                for j in range(8):
                    SW(t, "hs", 8 * u + j + 1)
                    t.transpose(pt[:, 1024 + 128 * j:1024 + 128 * (j + 1)],
                                _b(hs_o, 2048)[:, 256 * j + 128 * h:256 * j + 128 * (h + 1)],
                                identb[:, :]).then_inc(sems["tphs"], 1)

            hn_burst(0, 0)
            hn_burst(0, 1)

            for st in range(S):
                G0 = [0, 8, 16, 24]
                if st == 0:
                    for idx, m in enumerate(G0):
                        SW(t, "gev", 32 * st + idx - 3)
                        for k in range(8):
                            wu_wait(t, k)
                            if idx == 0:
                                SW(t, "hn", 8 * st + k + 1)
                                SW(t, "hn1", 8 * st + k + 1)
                            mm = t.matmul(gslot(idx), wu(k)[:, 128 * m:128 * (m + 1)],
                                          hnT(0)[:, 256 * k:256 * (k + 1)],
                                          start=(k == 0), stop=(k == 7))
                            if k == 7:
                                mm.then_inc(sems["gmm"], 1)
                else:
                    for idx, m in enumerate(G0):
                        SW(t, "gev", 32 * st + idx - 3)
                        for k in range(8):
                            wu_wait(t, k)
                            if idx == 0:
                                SW(t, "hn", 8 * st + k + 1)
                                SW(t, "hn1", 8 * st + k + 1)
                            t.matmul(gslot(idx), wu(k)[:, 128 * m:128 * (m + 1)],
                                     hnT(st % 2)[:, 256 * k:256 * (k + 1)],
                                     start=(k == 0), stop=False)
                    for idx, m in enumerate(G0):
                        for k in range(8, 16):
                            wu_wait(t, k)
                            if idx == 0:
                                SW(t, "cv", 8 * (st - 1) + (k - 8) + 1)
                            mm = t.matmul(gslot(idx), wu(k)[:, 128 * m:128 * (m + 1)],
                                          hT((st - 1) % 2)[:, 256 * (k - 8):256 * (k - 7)],
                                          start=False, stop=(k == 15))
                            if k == 15:
                                mm.then_inc(sems["gmm"], 1)

                for j in range(1, 8):
                    for idx, m in enumerate([j, j + 8, j + 16, j + 24]):
                        p = 4 * j + idx
                        SW(t, "gev", 32 * st + p - 3)
                        ks = range(8) if st == 0 else range(16)
                        for k in ks:
                            rhs = (hnT(st % 2)[:, 256 * k:256 * (k + 1)] if k < 8 else
                                   hT((st - 1) % 2)[:, 256 * (k - 8):256 * (k - 7)])
                            mm = t.matmul(gslot(p), wu(k)[:, 128 * m:128 * (m + 1)],
                                          rhs, start=(k == 0),
                                          stop=(k == (7 if st == 0 else 15)))
                            if k == (7 if st == 0 else 15):
                                mm.then_inc(sems["gmm"], 1)
                    if j == 1 and st >= 1:
                        hs_burst(st - 1, 0)
                    if j == 3 and st >= 1:
                        hs_burst(st - 1, 1)
                    if st <= S - 2 and j == 5:
                        hn_burst(st + 1, 0)
                        hn_burst(st + 1, 1)

            hs_burst(S - 1, 0)
            hs_burst(S - 1, 1)

            SW(t, "xn", 2 * S)
            SW(t, "cupd", 8 * S)
            for oi in range(8):
                SW(t, "wq", oi + 1)
                SW(t, "wqe", 8 * oi)
                for j in range(8):
                    t.transpose(pt[:, 1024 + 128 * j:1024 + 128 * (j + 1)],
                                _b(xq_o[0], 1024)[:, 128 * j:128 * (j + 1)],
                                identb[:, :]).then_inc(sems["wqt"], 1)

            def bcast(u_, src_c):
                for n in range(2):
                    t.matmul(pf2[:, 512 * n:512 * (n + 1)],
                             identb[:16, u_:u_ + 1].broadcast_to((16, 128)),
                             modnatb[:16, src_c + 512 * n:src_c + 512 * (n + 1)],
                             start=True, stop=True).then_inc(sems["fbc"], 1)

            def bc_trio(u_):
                SW(t, "fbe", 3 * u_)
                bcast(u_, 1024)
                SW(t, "fbe", 3 * u_ + 1)
                bcast(u_, 0)
                SW(t, "fbe", 3 * u_ + 2)
                bcast(u_, 2048)

            for k in range(2 * S + 1):
                if k == 0:
                    bc_trio(0)
                if k >= 1:
                    j = k - 1
                    par = j % 2
                    SW(t, "fq", k)
                    SW(t, "gq", k)
                    SW(t, "fxq", j)
                    for jj in range(8):
                        t.transpose(pt[:, 128 * jj:128 * (jj + 1)],
                                    _b(xq_o[par], 1024)[:, 128 * jj:128 * (jj + 1)],
                                    identb[:, :]).then_inc(sems["ftp"], 1)
                    SW(t, "fxq", k)
                    if j >= 2:
                        SW(t, "fout", j - 1)
                    for n in range(2):
                        for kk in range(8):
                            t.matmul(pf[:, 1024 * par + 512 * n:1024 * par + 512 * (n + 1)],
                                     _b(xqT_o[par], 1024)[:, 128 * kk:128 * (kk + 1)],
                                     _b(wqT_o + 1024 * kk, 1024)[:, 512 * n:512 * (n + 1)],
                                     start=(kk == 0), stop=False)
                        t.matmul(pf[:, 1024 * par + 512 * n:1024 * par + 512 * (n + 1)],
                                 ones1[:, :], fbias[:, 512 * n:512 * (n + 1)],
                                 start=False, stop=True).then_inc(sems["fmm"], 1)
                if k % 2 == 1 and (k + 1) // 2 <= S - 1:
                    bc_trio((k + 1) // 2)

        @blk.scalar
        def _(s):
            stn = [0]

            def stats_sq(u, h):
                SW(s, "xi%d%d" % (u % 3, h), 16 * (u // 3 + 1))
                SW(s, "tphn", 16 * (u - 1) + 16)
                if u >= 1:
                    SW(s, "stat", 2 * (u - 1) + h + 1)
                s.activation(_b(xln_o + 1024 * h, 1024),
                             _c(xin_o[u % 3] + 1024 * h, 1024), AF.Square,
                             accum_out=sm(C_S2 + h))
                s.activation(_b(xln_o + 1024 * h, 1024),
                             _c(xin_o[u % 3] + 1024 * h, 1024), AF.Identity,
                             accum_out=sm(C_S1 + h)).then_inc(sems["sq"], 1)

            def rinv_chain(u, h):
                SW(s, "stat", 2 * u + h + 1)
                s.activation(sm(C_VAR + h), sm(C_VAR + h), AF.Ln,
                             bias=f32(EPS)).then_inc(sems["st2"], 1)
                stn[0] += 1
                SW(s, "st2", stn[0])
                s.activation(sm(C_RINV + h), sm(C_VAR + h), AF.Exp,
                             scale=-0.5).then_inc(sems["rinv"], 1)
                stn[0] += 1
                s.nop().then_inc(sems["st2"], 1)

            def normalize(u, h):
                SW(s, "nm", 2 * u + h + 1)
                SW(s, "st2", stn[0])
                s.activation(_b(xln_o + 1024 * h, 1024),
                             _c(xin_o[u % 3] + 1024 * h, 1024), AF.Identity,
                             bias=sm(C_NM + h), scale=sm(C_RINV + h)).then_inc(sems["ln"], 1)

            def hn_evacs(u, h):
                SW(s, "tphn", 16 * u + 8 * h + 8)
                base = 0 if h == 0 else 1024
                for j in range(8):
                    s.activation(hnT(u % 2)[:, 256 * j + 128 * h:256 * j + 128 * (h + 1)],
                                 pt[:, base + 128 * j:base + 128 * (j + 1)], AF.Identity,
                                 bias=modT[:, 16 * j + u:16 * j + u + 1],
                                 scale=modT[:, 16 * (8 + j) + u:16 * (8 + j) + u + 1]
                                 ).then_inc(sems["hn1" if h else "hn"], 1)

            for h in (0, 1):
                stats_sq(0, h)
            for h in (0, 1):
                rinv_chain(0, h)
                normalize(0, h)
            hn_evacs(0, 0)
            hn_evacs(0, 1)

            for st in range(S):
                for j in range(8):
                    SW(s, "gmm", 32 * st + 4 * (j + 1))
                    SW(s, "cv", 8 * st + j - 1)
                    for role in range(4):
                        p = 4 * j + role
                        func = AF.Tanh if role == 3 else AF.Sigmoid
                        m = role * 8 + j
                        s.activation(gst_sl(role, j), gslot(p), func,
                                     bias=biasT[:, m:m + 1]).then_inc(sems["gev"], 1)
                    if j >= 1:
                        SW(s, "cupd", 8 * st + j)
                        s.activation(_b(ctt_o + 256 * ((j - 1) % 2), 256),
                                     _c(cT_o + 256 * (j - 1), 256), AF.Tanh
                                     ).then_inc(sems["tanh"], 1)
                    if st <= S - 2:
                        u = st + 1
                        if j == 0:
                            stats_sq(u, 0)
                        elif j == 1:
                            stats_sq(u, 1)
                        elif j == 2:
                            rinv_chain(u, 0)
                        elif j == 3:
                            rinv_chain(u, 1)
                        elif j == 4:
                            normalize(u, 0)
                            normalize(u, 1)
                        elif j == 5:
                            hn_evacs(u, 0)
                        elif j == 6:
                            hn_evacs(u, 1)
                SW(s, "cupd", 8 * st + 8)
                s.activation(_b(ctt_o + 256 * (7 % 2), 256), _c(cT_o + 256 * 7, 256),
                             AF.Tanh).then_inc(sems["tanh"], 1)

            for oi in range(8):
                par = oi % 2
                SW(s, "wqa", oi + 1)
                s.activation(fs(F_L, par), fs(F_AMX, par), AF.Ln).then_inc(sems["wql"], 1)
                SW(s, "wqu", oi + 1)
                s.copy(ftip(par), fs(F_U, par)).then_inc(sems["st2"], 1)
                stn[0] += 1
                SW(s, "st2", stn[0])
                s.copy(fs(F_TF, par), ftip(par)).then_inc(sems["wqi"], 1)
                SW(s, "wqx", oi + 1)
                s.activation(scbp(par), fs(F_EB, par), AF.Exp, scale=LN2C)
                s.activation(fs(F_RS, par), fs(F_EB, par), AF.Exp,
                             scale=-LN2C).then_inc(sems["wqc"], 1)

            def s_st2(op):
                op.then_inc(sems["st2"], 1)
                stn[0] += 1
                SW(s, "st2", stn[0])

            for k in range(2 * S + 1):
                if k == 0:
                    SW(s, "fbc", 2)
                    s.copy(cbS, pf2[:, :]).then_inc(sems["fbe"], 1)
                    SW(s, "fbc", 4)
                    s.copy(sbS, pf2[:, :]).then_inc(sems["fbe"], 1)
                    SW(s, "fbc", 6)
                    SW(s, "xst0", 32 * 8)
                    SW(s, "xst1", 32 * 8)
                    s.copy(_c(bgb_o, 1024), pf2[:, :]).then_inc(sems["fbe"], 1)
                if k <= 2 * S - 1:
                    s_, h_, par = k // 2, k % 2, k % 2
                    SW(s, "rb%d" % par, 16 * (k // 2 + 1))
                    if k >= 2:
                        SW(s, "fmm", 2 * (k - 1))
                    s.activation(_b(xqT_o[par], 1024), _c(xr4_o[k % 4], 1024), AF.Square,
                                 accum_out=sm(C_S2))
                    s_st2(s.activation(_b(xqT_o[par], 1024), _c(xr4_o[k % 4], 1024),
                                       AF.Identity, accum_out=sm(C_S1)))
                    s.activation(sm(C_MS), sm(C_S2), AF.Identity, scale=1.0 / 1024.0)
                    s_st2(s.activation(sm(C_MEAN), sm(C_S1), AF.Identity,
                                       scale=-1.0 / 1024.0))
                    s_st2(s.activation(sm(C_MSQ), sm(C_MEAN), AF.Square))
                    s_st2(s.activation(sm(C_VAR), sm(C_MSQ), AF.Identity, scale=-1.0,
                                       bias=sm(C_MS)))
                    s_st2(s.activation(sm(C_VAR), sm(C_VAR), AF.Ln, bias=f32(EPS)))
                    s_st2(s.activation(sm(C_RINV), sm(C_VAR), AF.Exp, scale=-0.5))
                    s_st2(s.activation(sm(C_NM), sm(C_MEAN), AF.Identity,
                                       scale=sm(C_RINV)))
                    SW(s, "fmo", k)
                    s.activation(_c(tln_o, 1024), _c(xr4_o[k % 4], 1024), AF.Identity,
                                 bias=sm(C_NM), scale=sm(C_RINV)).then_inc(sems["fno"], 1)
                    if k >= 1:
                        SW(s, "ftp", 8 * k)
                        s.copy(_b(xqT_o[(k - 1) % 2], 1024),
                               pt[:, 0:1024]).then_inc(sems["fxq"], 1)
                    SW(s, "fam", k + 1)
                    s.activation(fs(F_L, par), fs(F_AMX, par), AF.Ln).then_inc(sems["fln"], 1)
                    SW(s, "fu", k + 1)
                    s_st2(s.copy(ftip(par), fs(F_U, par)))
                    s.copy(fs(F_TF, par), ftip(par)).then_inc(sems["fif"], 1)
                    SW(s, "fex", k + 1)
                    if k >= 1:
                        SW(s, "gq", k - 1)
                    s.activation(scbp(par), fs(F_EB, par), AF.Exp, scale=LN2C)
                    s.activation(fs(F_RS, par), fs(F_EB, par), AF.Exp,
                                 scale=-LN2C).then_inc(sems["fsc"], 1)
                if k == 2 * S:
                    SW(s, "ftp", 8 * k)
                    s.copy(_b(xqT_o[(k - 1) % 2], 1024),
                           pt[:, 0:1024]).then_inc(sems["fxq"], 1)
                if k % 2 == 1:
                    sg = (k - 1) // 2
                    if sg >= 1:
                        SW(s, "fout", 2 * sg)
                        SW(s, "fbc", 6 * sg + 6)
                        s.copy(_c(bgb_o, 1024), pf2[:, :]).then_inc(sems["fbe"], 1)
                    sp = (k + 1) // 2
                    if sp <= S - 1:
                        SW(s, "fmo", k + 1)
                        SW(s, "fbc", 6 * sp + 2)
                        s.copy(cbS, pf2[:, :]).then_inc(sems["fbe"], 1)
                        SW(s, "fbc", 6 * sp + 4)
                        s.copy(sbS, pf2[:, :]).then_inc(sems["fbe"], 1)

        @blk.vector
        def _(v):
            vtn = [7]

            def stats_v(u, h):
                SW(v, "sq", 2 * u + h + 1)
                v.tensor_scalar(sm(C_MS + h), sm(C_S2 + h), 1.0 / 1024.0, None, ALU.mult)
                v.tensor_scalar(sm(C_MEAN + h), sm(C_S1 + h), 1.0 / 1024.0, None,
                                ALU.mult)
                v.scalar_tensor_tensor(sm(C_MSQ + h), sm(C_S1 + h), 1.0 / 1048576.0,
                                       sm(C_S1 + h), ALU.mult, ALU.mult).then_inc(sems["vt"], 1)
                vtn[0] += 1
                SW(v, "vt", vtn[0])
                v.tensor_sub(sm(C_VAR + h), sm(C_MS + h), sm(C_MSQ + h)).then_inc(sems["stat"], 1)

            def nm_v(u, h):
                SW(v, "rinv", 2 * u + h + 1)
                v.scalar_tensor_tensor(sm(C_NM + h), sm(C_MEAN + h), -1.0,
                                       sm(C_RINV + h), ALU.mult, ALU.mult).then_inc(sems["nm"], 1)

            for h in (0, 1):
                stats_v(0, h)
            for h in (0, 1):
                nm_v(0, h)

            ctmp = pf2[:, 0:256]

            def hs_v(st, jj):
                SW(v, "tphs", 16 * (st - 1) + 8 + jj + 1)
                v.tensor_scalar(_b(hs_o + 256 * jj, 256),
                                hT(st % 2)[:, 256 * jj:256 * (jj + 1)],
                                modT[:, 16 * (16 + jj) + st:16 * (16 + jj) + st + 1],
                                None, ALU.mult).then_inc(sems["hs"], 1)

            for st in range(S):
                for j in range(8):
                    if st >= 1 and j == 4:
                        u = st - 1
                        SW(v, "tphs", 16 * u + 16)
                        v.tensor_tensor(_c(xin_o[u % 3] + 1024, 1024),
                                        _c(xin_o[u % 3] + 1024, 1024),
                                        pt[:, 1024:2048], ALU.add).then_inc(sems["xn"], 1)
                    if st <= S - 2:
                        if j == 1:
                            stats_v(st + 1, 0)
                        elif j == 2:
                            stats_v(st + 1, 1)
                            nm_v(st + 1, 0)
                        elif j == 3:
                            nm_v(st + 1, 1)
                    SW(v, "gev", 32 * st + 4 * j + 4)
                    ig = gst_sl(0, j)
                    fg = gst_sl(1, j)
                    og = gst_sl(2, j)
                    cc = gst_sl(3, j)
                    cT_j = _c(cT_o + 256 * j, 256)
                    if st >= 1:
                        v.tensor_tensor(ctmp, fg, cT_j, ALU.mult)
                        v.tensor_tensor(cT_j, ig, cc, ALU.mult).then_inc(sems["vt"], 1)
                        vtn[0] += 1
                        SW(v, "vt", vtn[0])
                        v.tensor_tensor(cT_j, cT_j, ctmp, ALU.add).then_inc(sems["cupd"], 1)
                    else:
                        v.tensor_tensor(cT_j, ig, cc, ALU.mult).then_inc(sems["cupd"], 1)
                    SW(v, "tanh", 8 * st + j + 1)
                    v.tensor_tensor(hT(st % 2)[:, 256 * j:256 * (j + 1)], og,
                                    _b(ctt_o + 256 * (j % 2), 256), ALU.mult
                                    ).then_inc(sems["cv"], 1)
                    if j >= 3:
                        hs_v(st, j - 3)
                    if st >= 1 and j == 2:
                        u = st - 1
                        SW(v, "tphs", 16 * u + 8)
                        v.tensor_tensor(_c(xin_o[u % 3], 1024),
                                        _c(xin_o[u % 3], 1024),
                                        pt[:, 1024:2048], ALU.add).then_inc(sems["xn"], 1)
                v.engine_nop().then_inc(sems["vt"], 1)
                vtn[0] += 1
                SW(v, "vt", vtn[0])
                for jj in (5, 6, 7):
                    hs_v(st, jj)

            u = S - 1
            for h in (0, 1):
                SW(v, "tphs", 16 * u + 8 * (h + 1))
                v.tensor_tensor(_c(xin_o[u % 3] + 1024 * h, 1024),
                                _c(xin_o[u % 3] + 1024 * h, 1024),
                                pt[:, 1024:2048], ALU.add).then_inc(sems["xn"], 1)

            def eb_chain(par, dref, outsem):
                v.tensor_tensor(fs(F_CM, par), fs(F_U, par), fs(F_TF, par),
                                ALU.is_lt).then_inc(sems["vt"], 1)
                vtn[0] += 1
                SW(v, "vt", vtn[0])
                v.tensor_tensor(fs(F_EB, par), fs(F_TF, par), fs(F_CM, par),
                                ALU.subtract).then_inc(sems["vt"], 1)
                vtn[0] += 1
                v.tensor_scalar(fs(F_CM, par), fs(F_AMAX, par), 1e-38, None,
                                ALU.is_ge).then_inc(sems["vt"], 1)
                vtn[0] += 1
                SW(v, "vt", vtn[0])
                v.tensor_scalar(fs(F_EB, par), fs(F_EB, par), -160.0, None,
                                ALU.add).then_inc(sems["vt"], 1)
                vtn[0] += 1
                SW(v, "vt", vtn[0])
                v.tensor_tensor(fs(F_EB, par), fs(F_EB, par), fs(F_CM, par),
                                ALU.mult).then_inc(sems["vt"], 1)
                vtn[0] += 1
                SW(v, "vt", vtn[0])
                if dref[0] == "s":
                    v.tensor_scalar(fs(F_EB, par), fs(F_EB, par), dref[1], None,
                                    ALU.add).then_inc(sems[outsem], 1)
                else:
                    v.tensor_tensor(fs(F_EB, par), fs(F_EB, par), dref[1],
                                    ALU.add).then_inc(sems[outsem], 1)


            def blockchain_head(src, par):
                v.tensor_reduce(
                    fs(F_AMAX, par).rearrange("p (b o) -> p b o", o=1),
                    v3(src), AXX, ALU.max, apply_absolute_value=True).then_inc(sems["vt"], 1)
                vtn[0] += 1
                SW(v, "vt", vtn[0])
                v.tensor_scalar(fs(F_AMX, par), fs(F_AMAX, par), 1e-30, None, ALU.max)

            for oi in range(8):
                par = oi % 2
                SW(v, "wqd%d" % par, 16 * (oi // 2 + 1))
                if oi >= 2:
                    SW(v, "wql", oi - 1)
                wf_ap = _b(wf_o[par], 1024)
                blockchain_head(wf_ap, par)
                v.engine_nop().then_inc(sems["wqa"], 1)
                SW(v, "wql", oi + 1)
                v.tensor_scalar(fs(F_U, par), fs(F_L, par), 1.0 / LN2C, 158.0,
                                ALU.mult, ALU.add).then_inc(sems["wqu"], 1)
                SW(v, "wqi", oi + 1)
                eb_chain(par, ("t", dwh[:, 32 * oi:32 * (oi + 1)]), "wqx")
                SW(v, "wqc", oi + 1)
                SW(v, "wqt", 8 * oi)
                stair_ops(v, wf_ap, _b(xq_o[0], 1024), _b(tA_o, 1024), _b(tB_o, 1024),
                          _b(tC_o, 1024), _b(tD2_o, 1024), par, 0, 32)
                v.engine_nop().then_inc(sems["wq"], 1)
                SW(v, "wqt", 8 * (oi + 1))
                for j in range(8):
                    v.tensor_copy(_b(wqT_o + 1024 * j + 128 * oi, 128),
                                  pt[:, 1024 + 128 * j:1024 + 128 * (j + 1)]
                                  ).then_inc(sems["wqe"], 1)

            for k in range(2 * S + 2):
                if 1 <= k <= 2 * S:
                    j = k - 1
                    par = j % 2
                    SW(v, "fsc", j + 1)
                    SW(v, "ftp", 8 * (j - 1))
                    if j == 0:
                        SW(v, "wqt", 64)
                    stair_ops(v, _c(h2_o, 1024), _b(xq_o[par], 1024), _b(tA_o, 1024),
                              _b(tB_o, 1024), _b(tC_o, 1024), _b(tD_o, 1024), par, 0, 20)
                    v.engine_nop().then_inc(sems["fq"], 1)
                if k >= 2:
                    j = k - 2
                    par = j % 2
                    SW(v, "fbe", 3 * (j // 2) + 3)
                    SW(v, "fmm", 2 * (j + 1))
                    gf = _b(wf_o[1], 1024)
                    v.tensor_tensor(gf, pf[:, 1024 * par:1024 * (par + 1)],
                                    _c(bgb_o, 1024), ALU.mult)
                    v.tensor_tensor(_c(xr4_o[j % 4], 1024), gf, _c(xr4_o[j % 4], 1024),
                                    ALU.add).then_inc(sems["fout"], 1)
                if k <= 2 * S - 1:
                    s_, h_, par = k // 2, k % 2, k % 2
                    SW(v, "fno", k + 1)
                    SW(v, "fbe", 2 if s_ == 0 else 3 * s_ + 2)
                    SW(v, "gq", k)
                    h2 = _c(h2_o, 1024)
                    v.tensor_tensor(h2, _c(tln_o, 1024), cbS, ALU.mult)
                    v.tensor_tensor(h2, h2, sbS, ALU.add).then_inc(sems["fmo"], 1)
                    if k >= 2:
                        SW(v, "fln", k - 1)
                    else:
                        SW(v, "wql", 8)
                    blockchain_head(h2, par)
                    v.engine_nop().then_inc(sems["fam"], 1)
                    SW(v, "fln", k + 1)
                    v.tensor_scalar(fs(F_U, par), fs(F_L, par), 1.0 / LN2C, 158.0,
                                    ALU.mult, ALU.add).then_inc(sems["fu"], 1)
                    SW(v, "fif", k + 1)
                    eb_chain(par, ("s", dhalf[:, :]), "fex")

        @blk.sync
        def _(sync):
            for uu in (0, 1, 2):
                for h in (0, 1):
                    sync.dma_start(out=_c(xin_o[uu % 3] + 1024 * h, 1024),
                                   in_=x_d[uu, 128 * h:128 * (h + 1), :]
                                   ).then_inc(sems["xi%d%d" % (uu % 3, h)], 16)
            for st in range(S):
                if st >= 1:
                    u = st - 1
                    for h in (0, 1):
                        SW(sync, "xn", 2 * u + h + 1)
                        sync.dma_start(out=xnew_d[u, 128 * h:128 * (h + 1), :],
                                       in_=_c(xin_o[u % 3] + 1024 * h, 1024)
                                       ).then_inc(sems["xst%d" % (u % 2)], 16)
                uu = st + 2
                if 3 <= uu <= S - 1:
                    SW(sync, "xst%d" % ((uu - 3) % 2), 32 * ((uu - 3) // 2 + 1))
                    for h in (0, 1):
                        sync.dma_start(out=_c(xin_o[uu % 3] + 1024 * h, 1024),
                                       in_=x_d[uu, 128 * h:128 * (h + 1), :]
                                       ).then_inc(sems["xi%d%d" % (uu % 3, h)], 16)
            u = S - 1
            for h in (0, 1):
                SW(sync, "xn", 2 * u + h + 1)
                sync.dma_start(out=xnew_d[u, 128 * h:128 * (h + 1), :],
                               in_=_c(xin_o[u % 3] + 1024 * h, 1024)
                               ).then_inc(sems["xst%d" % (u % 2)], 16)

            SW(sync, "xst0", 32 * 8)
            SW(sync, "xst1", 32 * 8)
            for i in range(4):
                sync.dma_start(out=_c(xr4_o[i % 4], 1024),
                               in_=xnew_d[i // 2, 128 * (i % 2):128 * (i % 2 + 1), :]
                               ).then_inc(sems["rb%d" % (i % 2)], 16)
            for i in range(2 * S):
                s_, h_ = i // 2, i % 2
                SW(sync, "fout", i + 1)
                sync.dma_start(out=out_d[s_, 128 * h_:128 * (h_ + 1), :],
                               in_=_c(xr4_o[i % 4], 1024)).then_inc(sems["ost%d" % (i % 2)], 16)
                if i + 4 < 2 * S:
                    jj = i + 4
                    SW(sync, "ost%d" % (i % 2), 16 * (i // 2 + 1))
                    sync.dma_start(out=_c(xr4_o[jj % 4], 1024),
                                   in_=xnew_d[jj // 2, 128 * (jj % 2):128 * (jj % 2 + 1), :]
                                   ).then_inc(sems["rb%d" % (jj % 2)], 16)
            sync.wait_ge(sems["ost0"], 16 * S)
            sync.wait_ge(sems["ost1"], 16 * S)
    return nc


def kernel(L0_attn_input_seq, L0_attn_w, L0_attn_u, L0_attn_bias,
           L0_ffn_weight, L0_ffn_bias, L0_ffn_delta_w, L0_ffn_delta_act,
           L0_c, L0_wmod):
    x = np.ascontiguousarray(np.asarray(L0_attn_input_seq, dtype=np.float32))
    w = np.ascontiguousarray(np.asarray(L0_attn_w, dtype=np.float32))
    u = np.ascontiguousarray(np.asarray(L0_attn_u, dtype=np.float32))
    bias = np.ascontiguousarray(np.asarray(L0_attn_bias, dtype=np.float32))
    wffn = np.ascontiguousarray(np.asarray(L0_ffn_weight, dtype=np.float32))
    bffn = np.ascontiguousarray(np.asarray(L0_ffn_bias, dtype=np.float32)).reshape(1, -1)
    dw = np.ascontiguousarray(np.asarray(L0_ffn_delta_w, dtype=np.float32))
    dact = np.ascontiguousarray(np.asarray(L0_ffn_delta_act, dtype=np.float32)).reshape(1, 1)
    cmod = np.ascontiguousarray(np.asarray(L0_c, dtype=np.float32))
    wmod = np.ascontiguousarray(np.asarray(L0_wmod, dtype=np.float32))

    identb = np.eye(128, dtype=ml_dtypes.bfloat16)
    identf = np.eye(128, dtype=np.float32)
    ones1 = np.ones((1, 128), dtype=ml_dtypes.bfloat16)

    nc = build_program()
    in_maps = []
    for c in range(NCORES):
        xs = np.ascontiguousarray(x[:, B * c:B * (c + 1), :])
        in_maps.append(dict(x=xs, w_attn=w, u_attn=u, bias_attn=bias,
                            wmod=wmod, cmod=cmod, wffn=wffn, bffn=bffn,
                            dw=dw, dact=dact, identb=identb, identf=identf,
                            ones1=ones1))
    res = run_bass_kernel_spmd(nc, in_maps, core_ids=list(range(NCORES)))
    return np.concatenate([r["out"] for r in res.results], axis=1)


# version 15
# speedup vs baseline: 1.7852x; 1.7852x over previous
import numpy as np
import ml_dtypes
import concourse.bass as bass
import concourse.mybir as mybir
from concourse.bass_utils import run_bass_kernel_spmd

F32 = mybir.dt.float32
BF16 = mybir.dt.bfloat16
I32 = mybir.dt.int32
AF = mybir.ActivationFunctionType
ALU = mybir.AluOpType
AXX = mybir.AxisListType.X

NCORES = 8
S = 16
B = 256
D = 1024
LN2C = float(np.log(2.0))
EPS = 1e-5
f32 = np.float32


def build_program():
    nc = bass.Bass("TRN2", target_bir_lowering=False, debug=False,
                   detect_race_conditions=False)

    x_d = nc.dram_tensor("x", [S, B, D], F32, kind="ExternalInput")
    w_d = nc.dram_tensor("w_attn", [D, 4 * D], F32, kind="ExternalInput")
    u_d = nc.dram_tensor("u_attn", [D, 4 * D], F32, kind="ExternalInput")
    bias_d = nc.dram_tensor("bias_attn", [4 * D], F32, kind="ExternalInput")
    wmod_d = nc.dram_tensor("wmod", [D, 6 * D], F32, kind="ExternalInput")
    cmod_d = nc.dram_tensor("cmod", [S, D], F32, kind="ExternalInput")
    wffn_d = nc.dram_tensor("wffn", [D, D], F32, kind="ExternalInput")
    bffn_d = nc.dram_tensor("bffn", [1, D], F32, kind="ExternalInput")
    dw_d = nc.dram_tensor("dw", [D, 32], F32, kind="ExternalInput")
    dact_d = nc.dram_tensor("dact", [1, 1], F32, kind="ExternalInput")
    identb_d = nc.dram_tensor("identb", [128, 128], BF16, kind="ExternalInput")
    identf_d = nc.dram_tensor("identf", [128, 128], F32, kind="ExternalInput")
    ones_d = nc.dram_tensor("ones1", [1, 128], BF16, kind="ExternalInput")
    xnew_d = nc.dram_tensor("xnew_h", [S, B, D], F32, kind="Internal")
    out_d = nc.dram_tensor("out", [S, B, D], F32, kind="ExternalOutput")

    arena_w = nc.alloc_sbuf_tensor("arena_w", [128, 16 * 4096], BF16)

    def wu(k):
        return arena_w[:, 4096 * k:4096 * (k + 1)]

    def wmod_t(kk):
        off = 4096 * 4 + 6144 * kk
        return arena_w[:, off:off + 6144]

    AB = 17408
    arena_B = nc.alloc_sbuf_tensor("arena_B", [128, AB], BF16)

    def _b(off, n):
        return arena_B[:, off:off + n]
    gst_o = 0
    hnT_o = [4096, 6144]
    hT_o = [8192, 10240]
    ctt_o = 12288
    hs_o = 12800
    xln_o = 14848
    wqT_o = 0
    wf_o = [8192, 9216]
    xq_o = [10240, 11264]
    xqT_o = [12288, 13312]
    tA_o, tB_o, tC_o = 14336, 15360, 16384
    tD_o = wf_o[0]
    tD2_o = xq_o[1]
    csil_o = 0

    AC = 8192
    arena_C = nc.alloc_sbuf_tensor("arena_C", [128, AC], F32)

    def _c(off, n):
        return arena_C[:, off:off + n]
    cT_o = 0
    xin_o = [2048, 4096, 6144]
    xr_o = [0, 1024]
    tln_o = 2048
    h2_o = 3072
    outst_o = [4096, 5120]
    bgb_o = 6144
    xr4_o = [0, 1024, 4096, 5120]
    cin_o = 0
    modn_o = 1024

    cbS = arena_w[:, 0:1024]
    sbS = arena_w[:, 1024:2048]
    identb = nc.alloc_sbuf_tensor("identb_s", [128, 128], BF16)
    identf = nc.alloc_sbuf_tensor("identf_s", [128, 128], F32)
    ones1 = nc.alloc_sbuf_tensor("ones1_s", [1, 128], BF16)
    siluCT = nc.alloc_sbuf_tensor("siluCT", [128, 128], BF16)
    modT = nc.alloc_sbuf_tensor("modT", [128, 384], F32)
    modnatb = nc.alloc_sbuf_tensor("modnatb", [16, 3072], BF16)
    biasT = nc.alloc_sbuf_tensor("biasT", [128, 32], F32)
    fbias = nc.alloc_sbuf_tensor("fbias", [1, 1024], BF16)
    dwh = nc.alloc_sbuf_tensor("dwh", [128, 256], BF16)
    dhalf = nc.alloc_sbuf_tensor("dhalf", [128, 1], F32)
    dactb = nc.alloc_sbuf_tensor("dactb", [1, 1], BF16)
    smalls = nc.alloc_sbuf_tensor("smalls", [128, 24], F32)
    fsm = nc.alloc_sbuf_tensor("fsm", [128, 448], F32)
    fti = nc.alloc_sbuf_tensor("fti", [128, 64], I32)
    scb = nc.alloc_sbuf_tensor("scb", [128, 64], BF16)

    tmpm = arena_C[:, 7168:7296]
    tmpm2 = arena_C[:, 7296:7424]
    dwt = arena_C[:, 7424:7680]
    biasN = arena_C[:32, 7680:7808]

    C_S1, C_S2, C_MEAN, C_MS, C_MSQ, C_VAR, C_RINV, C_NM = 0, 2, 4, 6, 8, 10, 12, 14
    C_DACT, C_T4D = 16, 17

    def sm(col):
        return smalls[:, col:col + 1]

    F_AMAX, F_AMX, F_L, F_U, F_TF, F_CM, F_EB = 0, 32, 64, 96, 128, 160, 192
    F_RS = F_U

    def fs(c0, par=0):
        c = 224 * par + c0
        return fsm[:, c:c + 32]

    def ftip(par):
        return fti[:, 32 * par:32 * (par + 1)]

    def scbp(par):
        return scb[:, 32 * par:32 * (par + 1)]

    pf = nc.alloc_psum_tensor("pf", [128, 2048], F32)
    pf2 = nc.alloc_psum_tensor("pf2", [128, 1024], F32)
    pt = nc.alloc_psum_tensor("pt", [128, 2048], BF16)

    def gslot(p):
        i = p % 8
        c = 512 * (i % 4) + 256 * (i // 4)
        return pf[:, c:c + 256]

    sems = {}
    for name in ["pd", "wmod", "ww", "wu2", "wsm", "pr", "pr2", "xin00", "xin01", "xin10", "xin11",
                 "sq", "stat", "rinv", "nm",
                 "ln", "tphn", "hn", "gmm", "gev", "cupd", "tanh", "cv", "hs",
                 "tphs", "xn", "xst0", "xst1",
                 "wqd0", "wqd1", "wqa", "wql", "wqu", "wqi", "wqx", "wqc", "wq", "wqt",
                 "wqe", "rb0", "rb1", "fsq", "fst", "fri", "fnm", "fno", "fbc",
                 "fbgbe", "fmo", "fam", "fln", "fu", "fif", "fex", "fsc",
                 "fq", "ftp", "fxq", "fmm", "fout", "ost0", "ost1", "vt", "st2",
                 "fbe", "hn1", "gq",
                 "xi00", "xi01", "xi10", "xi11", "xi20", "xi21"]:
        sems[name] = nc.alloc_semaphore("s_" + name)

    def SW(eng, name, val):
        if val > 0:
            eng.wait_ge(sems[name], int(val))

    wu_waited = set()

    def wu_wait(eng, k):
        key = "ww" if k < 8 else "wu2"
        if key not in wu_waited:
            wu_waited.add(key)
            SW(eng, key, 128)

    def rep32(ap):
        return ap.rearrange("p (b o) -> p b o", o=1).broadcast_to((128, 32, 32))

    def v3(ap):
        return ap.rearrange("p (b i) -> p b i", i=32)

    def stair_ops(eng, src, xq_ap, tA, tB, tC, tD, par, b0, b1, use_stt=True):
        nb = b1 - b0
        c0, c1 = 32 * b0, 32 * b1

        def v3c(ap):
            return ap[:, c0:c1].rearrange("p (b i) -> p b i", i=32)

        def repc(ap):
            return ap[:, b0:b1].rearrange("p (b o) -> p b o", o=1
                                          ).broadcast_to((128, nb, 32))

        def sl(ap):
            return ap[:, c0:c1]

        def stt(out, in0, sc, in1, op0, op1):
            if use_stt:
                eng.scalar_tensor_tensor(out, in0, sc, in1, op0, op1)
            else:
                eng.tensor_scalar(out, in0, sc, None, op0)
                eng.tensor_tensor(out, out, in1, op1)

        eng.tensor_tensor(v3c(tA), v3c(src), repc(fs(F_RS, par)), ALU.mult)
        stt(sl(tB), sl(tA), -1.0, sl(tA), ALU.mult, ALU.max)
        eng.tensor_scalar(sl(tC), sl(tB), 0.25, None, ALU.is_ge)
        eng.tensor_scalar(sl(tD), sl(tB), 0.75, None, ALU.is_ge)
        eng.tensor_add(sl(tC), sl(tC), sl(tD))
        eng.tensor_scalar(sl(tD), sl(tB), 1.25, None, ALU.is_ge)
        eng.tensor_add(sl(tC), sl(tC), sl(tD))
        eng.tensor_scalar(sl(tD), sl(tB), 1.75, None, ALU.is_ge)
        eng.tensor_add(sl(tC), sl(tC), sl(tD))
        eng.tensor_scalar(sl(tD), sl(tB), 2.5, None, ALU.is_ge)
        eng.tensor_scalar(sl(tB), sl(tB), 3.5, None, ALU.is_ge)
        eng.tensor_add(sl(tD), sl(tD), sl(tB))
        stt(sl(tC), sl(tC), 0.5, sl(tD), ALU.mult, ALU.add)
        stt(sl(tD), sl(tA), -1.0, sl(tA), ALU.mult, ALU.max)
        eng.tensor_scalar(sl(tD), sl(tD), 5.0, None, ALU.is_ge)
        stt(sl(tC), sl(tD), 2.0, sl(tC), ALU.mult, ALU.add)
        eng.tensor_scalar(sl(tD), sl(tA), 0.0, 2.0, ALU.is_ge, ALU.mult)
        stt(sl(tD), sl(tD), -1.0, sl(tC), ALU.add, ALU.mult)
        eng.tensor_tensor(v3c(xq_ap), v3c(tD), repc(scbp(par)), ALU.mult)

    def stair_ops_pool(eng, src, xq_ap, tA, tB, tC, tD, par, b0, b1):
        nb = b1 - b0
        c0, c1 = 32 * b0, 32 * b1

        def v3c(ap):
            return ap[:, c0:c1].rearrange("p (b i) -> p b i", i=32)

        def repc(ap):
            return ap[:, b0:b1].rearrange("p (b o) -> p b o", o=1
                                          ).broadcast_to((128, nb, 32))

        def sl(ap):
            return ap[:, c0:c1]

        eng.tensor_tensor(v3c(tA), v3c(src), repc(fs(F_RS, par)), ALU.mult)
        eng.tensor_tensor(sl(tB), sl(tA), sl(tA), ALU.mult)
        eng.tensor_scalar(sl(tC), sl(tB), 0.0625, None, ALU.is_ge)
        eng.tensor_scalar(sl(tD), sl(tB), 0.5625, None, ALU.is_ge)
        eng.tensor_add(sl(tC), sl(tC), sl(tD))
        eng.tensor_scalar(sl(tD), sl(tB), 1.5625, None, ALU.is_ge)
        eng.tensor_add(sl(tC), sl(tC), sl(tD))
        eng.tensor_scalar(sl(tD), sl(tB), 3.0625, None, ALU.is_ge)
        eng.tensor_add(sl(tC), sl(tC), sl(tD))
        eng.tensor_scalar(sl(tC), sl(tC), 0.5, None, ALU.mult)
        eng.tensor_scalar(sl(tD), sl(tB), 6.25, None, ALU.is_ge)
        eng.tensor_add(sl(tC), sl(tC), sl(tD))
        eng.tensor_scalar(sl(tD), sl(tB), 12.25, None, ALU.is_ge)
        eng.tensor_add(sl(tC), sl(tC), sl(tD))
        eng.tensor_scalar(sl(tD), sl(tB), 25.0, 2.0, ALU.is_ge, ALU.mult)
        eng.tensor_add(sl(tC), sl(tC), sl(tD))
        eng.tensor_scalar(sl(tD), sl(tA), 0.0, 2.0, ALU.is_ge, ALU.mult)
        eng.tensor_scalar(sl(tD), sl(tD), -1.0, None, ALU.add)
        eng.tensor_tensor(sl(tD), sl(tD), sl(tC), ALU.mult)
        eng.tensor_tensor(v3c(xq_ap), v3c(tD), repc(scbp(par)), ALU.mult)

    with nc.Block() as blk:
        @blk.sync
        def _(sync):
            sync.dma_start(out=identb[:, :], in_=identb_d[:, :]).then_inc(sems["pd"], 16)
            sync.dma_start(out=identf[:, :], in_=identf_d[:, :]).then_inc(sems["pd"], 16)
            sync.dma_start(out=ones1[:, :], in_=ones_d[:, :]).then_inc(sems["pd"], 16)
            sync.dma_start(out=_c(cin_o, 1024)[:16, :], in_=cmod_d[:, :]).then_inc(sems["pd"], 16)
            sync.dma_start(out=biasN[:, :],
                           in_=bias_d[:].rearrange("(m p) -> m p", p=128)).then_inc(sems["pd"], 16)
            sync.dma_start(out=dwt[:, :].rearrange("p (a c) -> p a c", c=32),
                           in_=dw_d[:, :].rearrange("(a p) c -> p a c", p=128)).then_inc(sems["pd"], 16)

        @blk.gpsimd
        def _(g):
            for kk in range(8):
                g.dma_start(out=wmod_t(kk), in_=wmod_d[128 * kk:128 * (kk + 1), :]).then_inc(sems["wmod"], 16)
            for k in range(4):
                g.dma_start(out=wu(k), in_=w_d[128 * k:128 * (k + 1), :]).then_inc(sems["ww"], 16)
            g.dma_start(out=fbias[:, :], in_=bffn_d[:, :]).then_inc(sems["wsm"], 16)
            g.dma_start(out=dactb[:, :], in_=dact_d[:, :]).then_inc(sems["wsm"], 16)

    with nc.Block() as blk:
        @blk.scalar
        def _(s):
            SW(s, "pd", 96)
            s.activation(_b(csil_o, 1024)[:16, :], _c(cin_o, 1024)[:16, :],
                         AF.Silu).then_inc(sems["pr"], 1)
            SW(s, "pd", 96)
            s.activation(dwh[:, :], dwt[:, :], AF.Tanh, scale=4.0).then_inc(sems["pr"], 1)

        @blk.tensor
        def _(t):
            SW(t, "pr", 1)
            SW(t, "pd", 96)
            for kk in range(8):
                t.transpose(pt[:, 128 * kk:128 * kk + 16],
                            _b(csil_o, 1024)[:16, 128 * kk:128 * (kk + 1)],
                            identb[:16, :16]).then_inc(sems["pr2"], 1)
            SW(t, "pd", 96)
            t.transpose(pf[:, 0:32], biasN[:, :], identf[:32, :32]).then_inc(sems["pr2"], 1)
            SW(t, "wsm", 32)
            t.matmul(pf[:, 32:33], ones1[:, :], dactb[:, :],
                     start=True, stop=True).then_inc(sems["pr2"], 1)

        @blk.vector
        def _(v):
            SW(v, "pr2", 8)
            v.tensor_copy(
                siluCT[:, :].rearrange("p (j c) -> p j c", c=16),
                pt[:, :1024].rearrange("p (j c) -> p j c", c=128)[:, :, :16])
            SW(v, "pr2", 10)
            v.tensor_copy(biasT[:, :], pf[:, 0:32])
            v.tensor_copy(sm(C_DACT), pf[:, 32:33]).then_inc(sems["pr"], 1)
            v.tensor_scalar(dwh[:, :], dwh[:, :], 0.5, None, ALU.mult)
            SW(v, "pr", 4)
            v.tensor_scalar(dhalf[:, :], sm(C_T4D), 0.5, None, ALU.mult)

        @blk.scalar
        def _(s2_):
            SW(s2_, "pr", 3)
            s2_.activation(sm(C_T4D), sm(C_DACT), AF.Tanh,
                           scale=4.0).then_inc(sems["pr"], 1)


    with nc.Block(no_gpsimd_drain=True) as blk:
        @blk.tensor
        def _(t):
            SW(t, "wmod", 128)
            for c in range(12):
                reg = pf[:16, 512 * (c % 2):512 * (c % 2) + 512]
                if c >= 2:
                    SW(t, "pr", c + 3)
                for kk in range(8):
                    mm = t.matmul(reg, siluCT[:, 16 * kk:16 * (kk + 1)],
                                  wmod_t(kk)[:, 512 * c:512 * (c + 1)],
                                  start=(kk == 0), stop=(kk == 7))
                    if kk == 7:
                        mm.then_inc(sems["pr2"], 1)

        @blk.scalar
        def _(s):
            for c in range(12):
                SW(s, "pr2", 11 + c)
                s.copy(_c(modn_o, 6144)[:16, 512 * c:512 * (c + 1)],
                       pf[:16, 512 * (c % 2):512 * (c % 2) + 512]).then_inc(sems["pr"], 1)

    with nc.Block(no_gpsimd_drain=True) as blk:
        @blk.tensor
        def _(t):
            SW(t, "pr", 16)
            for jj in range(24):
                tp = t.transpose(pf[:, 1024 + 16 * jj:1024 + 16 * (jj + 1)],
                                 _c(modn_o, 6144)[:16, 128 * jj:128 * (jj + 1)],
                                 identf[:16, :16])
                if jj == 23:
                    tp.then_inc(sems["pr2"], 1)

        @blk.vector
        def _(v):
            SW(v, "pr2", 23)
            v.tensor_copy(
                modT[:, :].rearrange("p (j c) -> p j c", c=16),
                pf[:, 1024:1024 + 384].rearrange("p (j c) -> p j c", c=16))
            v.tensor_scalar(modT[:, 128:256], modT[:, 128:256], 1.0, None, ALU.add)
            ga = modT[:, 256:384]
            v.tensor_scalar(tmpm[:, :], ga, 0.0, None, ALU.is_ge)
            v.tensor_scalar(tmpm2[:, :], ga, 0.25, None, ALU.max)
            v.tensor_scalar(ga, ga, -0.25, None, ALU.min).then_inc(sems["vt"], 1)
            v.wait_ge(sems["vt"], 1)
            v.tensor_sub(tmpm2[:, :], tmpm2[:, :], ga).then_inc(sems["vt"], 1)
            v.wait_ge(sems["vt"], 2)
            v.tensor_mul(tmpm[:, :], tmpm[:, :], tmpm2[:, :]).then_inc(sems["vt"], 1)
            v.wait_ge(sems["vt"], 3)
            v.tensor_add(ga, ga, tmpm[:, :])
            nat = _c(modn_o, 6144)[:16, :]
            v.tensor_scalar(nat[:, 4096:5120], nat[:, 4096:5120], 1.0, None, ALU.add)
            gb = nat[:, 5120:6144]
            m16 = nat[:, 0:1024]
            a16 = nat[:, 1024:2048]
            v.tensor_scalar(m16, gb, 0.0, None, ALU.is_ge)
            v.tensor_scalar(a16, gb, 0.25, None, ALU.max)
            v.tensor_scalar(gb, gb, -0.25, None, ALU.min).then_inc(sems["vt"], 1)
            v.wait_ge(sems["vt"], 4)
            v.tensor_sub(a16, a16, gb).then_inc(sems["vt"], 1)
            v.wait_ge(sems["vt"], 5)
            v.tensor_mul(m16, m16, a16).then_inc(sems["vt"], 1)
            v.wait_ge(sems["vt"], 6)
            v.tensor_add(gb, gb, m16).then_inc(sems["vt"], 1)
            v.wait_ge(sems["vt"], 7)
            v.tensor_copy(modnatb[:, :], nat[:, 3072:6144])

    def gst_sl(role, j):
        return _b(gst_o + (role * 2 + (j % 2)) * 256, 256)

    def hnT(par):
        return _b(hnT_o[par], 2048)

    def hT(par):
        return _b(hT_o[par], 2048)

    with nc.Block() as blk:
        @blk.gpsimd
        def _(g):
            for k in range(4, 8):
                g.dma_start(out=wu(k), in_=w_d[128 * k:128 * (k + 1), :]).then_inc(sems["ww"], 16)
            for k in range(8):
                g.dma_start(out=wu(8 + k), in_=u_d[128 * k:128 * (k + 1), :]).then_inc(sems["wu2"], 16)
            SW(g, "gmm", 32 * S)
            for oi in range(8):
                SW(g, "wq", oi - 1)
                g.dma_start(out=_b(wf_o[oi % 2], 1024),
                            in_=wffn_d[128 * oi:128 * (oi + 1), :]).then_inc(sems["wqd%d" % (oi % 2)], 16)


        @blk.tensor
        def _(t):
            def hn_burst(u, h):
                SW(t, "ln", 2 * u + h + 1)
                if h == 0:
                    SW(t, "hn", 8 * u)
                else:
                    SW(t, "xn", 2 * u - 2)
                    SW(t, "hn1", 8 * u)
                base = 0 if h == 0 else 1024
                for j in range(8):
                    t.transpose(pt[:, base + 128 * j:base + 128 * (j + 1)],
                                _b(xln_o + 1024 * h, 1024)[:, 128 * j:128 * (j + 1)],
                                identb[:, :]).then_inc(sems["tphn"], 1)

            def hs_burst(u, h):
                SW(t, "xn", 2 * u + h)
                SW(t, "hn1", min(8 * (u + 1) + 8, 8 * S))
                for j in range(8):
                    SW(t, "hs", 8 * u + j + 1)
                    t.transpose(pt[:, 1024 + 128 * j:1024 + 128 * (j + 1)],
                                _b(hs_o, 2048)[:, 256 * j + 128 * h:256 * j + 128 * (h + 1)],
                                identb[:, :]).then_inc(sems["tphs"], 1)

            hn_burst(0, 0)
            hn_burst(0, 1)

            for st in range(S):
                G0 = [0, 8, 16, 24]
                if st == 0:
                    for idx, m in enumerate(G0):
                        SW(t, "gev", 32 * st + idx - 3)
                        for k in range(8):
                            wu_wait(t, k)
                            if idx == 0:
                                SW(t, "hn", 8 * st + k + 1)
                                SW(t, "hn1", 8 * st + k + 1)
                            mm = t.matmul(gslot(idx), wu(k)[:, 128 * m:128 * (m + 1)],
                                          hnT(0)[:, 256 * k:256 * (k + 1)],
                                          start=(k == 0), stop=(k == 7))
                            if k == 7:
                                mm.then_inc(sems["gmm"], 1)
                else:
                    for idx, m in enumerate(G0):
                        SW(t, "gev", 32 * st + idx - 3)
                        for k in range(8):
                            wu_wait(t, k)
                            if idx == 0:
                                SW(t, "hn", 8 * st + k + 1)
                                SW(t, "hn1", 8 * st + k + 1)
                            t.matmul(gslot(idx), wu(k)[:, 128 * m:128 * (m + 1)],
                                     hnT(st % 2)[:, 256 * k:256 * (k + 1)],
                                     start=(k == 0), stop=False)
                    for idx, m in enumerate(G0):
                        for k in range(8, 16):
                            wu_wait(t, k)
                            if idx == 0:
                                SW(t, "cv", 8 * (st - 1) + (k - 8) + 1)
                            mm = t.matmul(gslot(idx), wu(k)[:, 128 * m:128 * (m + 1)],
                                          hT((st - 1) % 2)[:, 256 * (k - 8):256 * (k - 7)],
                                          start=False, stop=(k == 15))
                            if k == 15:
                                mm.then_inc(sems["gmm"], 1)

                for j in range(1, 8):
                    for idx, m in enumerate([j, j + 8, j + 16, j + 24]):
                        p = 4 * j + idx
                        SW(t, "gev", 32 * st + p - 3)
                        ks = range(8) if st == 0 else range(16)
                        for k in ks:
                            rhs = (hnT(st % 2)[:, 256 * k:256 * (k + 1)] if k < 8 else
                                   hT((st - 1) % 2)[:, 256 * (k - 8):256 * (k - 7)])
                            mm = t.matmul(gslot(p), wu(k)[:, 128 * m:128 * (m + 1)],
                                          rhs, start=(k == 0),
                                          stop=(k == (7 if st == 0 else 15)))
                            if k == (7 if st == 0 else 15):
                                mm.then_inc(sems["gmm"], 1)
                    if j == 1 and st >= 1:
                        hs_burst(st - 1, 0)
                    if j == 3 and st >= 1:
                        hs_burst(st - 1, 1)
                    if st <= S - 2 and j == 5:
                        hn_burst(st + 1, 0)
                        hn_burst(st + 1, 1)

            hs_burst(S - 1, 0)
            hs_burst(S - 1, 1)

            SW(t, "xn", 2 * S)
            SW(t, "cupd", 8 * S)
            for oi in range(8):
                SW(t, "wq", oi + 1)
                SW(t, "wqe", 8 * oi)
                for j in range(8):
                    t.transpose(pt[:, 1024 + 128 * j:1024 + 128 * (j + 1)],
                                _b(xq_o[0], 1024)[:, 128 * j:128 * (j + 1)],
                                identb[:, :]).then_inc(sems["wqt"], 1)

            def bcast(u_, src_c):
                for n in range(2):
                    t.matmul(pf2[:, 512 * n:512 * (n + 1)],
                             identb[:16, u_:u_ + 1].broadcast_to((16, 128)),
                             modnatb[:16, src_c + 512 * n:src_c + 512 * (n + 1)],
                             start=True, stop=True).then_inc(sems["fbc"], 1)

            def bc_trio(u_):
                SW(t, "fbe", 3 * u_)
                bcast(u_, 1024)
                SW(t, "fbe", 3 * u_ + 1)
                bcast(u_, 0)
                SW(t, "fbe", 3 * u_ + 2)
                bcast(u_, 2048)

            for k in range(2 * S + 1):
                if k == 0:
                    bc_trio(0)
                if k >= 1:
                    j = k - 1
                    par = j % 2
                    SW(t, "fq", k)
                    SW(t, "fxq", j)
                    for jj in range(8):
                        t.transpose(pt[:, 128 * jj:128 * (jj + 1)],
                                    _b(xq_o[par], 1024)[:, 128 * jj:128 * (jj + 1)],
                                    identb[:, :]).then_inc(sems["ftp"], 1)
                    SW(t, "fxq", k)
                    if j >= 2:
                        SW(t, "fout", j - 1)
                    for n in range(2):
                        for kk in range(8):
                            t.matmul(pf[:, 1024 * par + 512 * n:1024 * par + 512 * (n + 1)],
                                     _b(xqT_o[par], 1024)[:, 128 * kk:128 * (kk + 1)],
                                     _b(wqT_o + 1024 * kk, 1024)[:, 512 * n:512 * (n + 1)],
                                     start=(kk == 0), stop=False)
                        t.matmul(pf[:, 1024 * par + 512 * n:1024 * par + 512 * (n + 1)],
                                 ones1[:, :], fbias[:, 512 * n:512 * (n + 1)],
                                 start=False, stop=True).then_inc(sems["fmm"], 1)
                if k % 2 == 1 and (k + 1) // 2 <= S - 1:
                    bc_trio((k + 1) // 2)

        @blk.scalar
        def _(s):
            stn = [0]

            def stats_sq(u, h):
                SW(s, "xi%d%d" % (u % 3, h), 16 * (u // 3 + 1))
                SW(s, "tphn", 16 * (u - 1) + 16)
                if u >= 1:
                    SW(s, "stat", 2 * (u - 1) + h + 1)
                s.activation(_b(xln_o + 1024 * h, 1024),
                             _c(xin_o[u % 3] + 1024 * h, 1024), AF.Square,
                             accum_out=sm(C_S2 + h))
                s.activation(_b(xln_o + 1024 * h, 1024),
                             _c(xin_o[u % 3] + 1024 * h, 1024), AF.Identity,
                             accum_out=sm(C_S1 + h)).then_inc(sems["sq"], 1)

            def rinv_chain(u, h):
                SW(s, "stat", 2 * u + h + 1)
                s.activation(sm(C_VAR + h), sm(C_VAR + h), AF.Ln,
                             bias=f32(EPS)).then_inc(sems["st2"], 1)
                stn[0] += 1
                SW(s, "st2", stn[0])
                s.activation(sm(C_RINV + h), sm(C_VAR + h), AF.Exp,
                             scale=-0.5).then_inc(sems["rinv"], 1)
                stn[0] += 1
                s.nop().then_inc(sems["st2"], 1)

            def normalize(u, h):
                SW(s, "nm", 2 * u + h + 1)
                SW(s, "st2", stn[0])
                s.activation(_b(xln_o + 1024 * h, 1024),
                             _c(xin_o[u % 3] + 1024 * h, 1024), AF.Identity,
                             bias=sm(C_NM + h), scale=sm(C_RINV + h)).then_inc(sems["ln"], 1)

            def hn_evacs(u, h):
                SW(s, "tphn", 16 * u + 8 * h + 8)
                base = 0 if h == 0 else 1024
                for j in range(8):
                    s.activation(hnT(u % 2)[:, 256 * j + 128 * h:256 * j + 128 * (h + 1)],
                                 pt[:, base + 128 * j:base + 128 * (j + 1)], AF.Identity,
                                 bias=modT[:, 16 * j + u:16 * j + u + 1],
                                 scale=modT[:, 16 * (8 + j) + u:16 * (8 + j) + u + 1]
                                 ).then_inc(sems["hn1" if h else "hn"], 1)

            for h in (0, 1):
                stats_sq(0, h)
            for h in (0, 1):
                rinv_chain(0, h)
                normalize(0, h)
            hn_evacs(0, 0)
            hn_evacs(0, 1)

            for st in range(S):
                for j in range(8):
                    SW(s, "gmm", 32 * st + 4 * (j + 1))
                    SW(s, "cv", 8 * st + j - 1)
                    for role in range(4):
                        p = 4 * j + role
                        func = AF.Tanh if role == 3 else AF.Sigmoid
                        m = role * 8 + j
                        s.activation(gst_sl(role, j), gslot(p), func,
                                     bias=biasT[:, m:m + 1]).then_inc(sems["gev"], 1)
                    if j >= 1:
                        SW(s, "cupd", 8 * st + j)
                        s.activation(_b(ctt_o + 256 * ((j - 1) % 2), 256),
                                     _c(cT_o + 256 * (j - 1), 256), AF.Tanh
                                     ).then_inc(sems["tanh"], 1)
                    if st <= S - 2:
                        u = st + 1
                        if j == 0:
                            stats_sq(u, 0)
                        elif j == 1:
                            stats_sq(u, 1)
                        elif j == 2:
                            rinv_chain(u, 0)
                        elif j == 3:
                            rinv_chain(u, 1)
                        elif j == 4:
                            normalize(u, 0)
                            normalize(u, 1)
                        elif j == 5:
                            hn_evacs(u, 0)
                        elif j == 6:
                            hn_evacs(u, 1)
                SW(s, "cupd", 8 * st + 8)
                s.activation(_b(ctt_o + 256 * (7 % 2), 256), _c(cT_o + 256 * 7, 256),
                             AF.Tanh).then_inc(sems["tanh"], 1)

            for oi in range(8):
                par = oi % 2
                SW(s, "wqa", oi + 1)
                s.activation(fs(F_L, par), fs(F_AMX, par), AF.Ln).then_inc(sems["wql"], 1)
                SW(s, "wqu", oi + 1)
                s.copy(ftip(par), fs(F_U, par)).then_inc(sems["st2"], 1)
                stn[0] += 1
                SW(s, "st2", stn[0])
                s.copy(fs(F_TF, par), ftip(par)).then_inc(sems["wqi"], 1)
                SW(s, "wqx", oi + 1)
                s.activation(scbp(par), fs(F_EB, par), AF.Exp, scale=LN2C)
                s.activation(fs(F_RS, par), fs(F_EB, par), AF.Exp,
                             scale=-LN2C).then_inc(sems["wqc"], 1)

            def s_st2(op):
                op.then_inc(sems["st2"], 1)
                stn[0] += 1
                SW(s, "st2", stn[0])

            for k in range(2 * S + 1):
                if k == 0:
                    SW(s, "fbc", 2)
                    s.copy(cbS, pf2[:, :]).then_inc(sems["fbe"], 1)
                    SW(s, "fbc", 4)
                    s.copy(sbS, pf2[:, :]).then_inc(sems["fbe"], 1)
                    SW(s, "fbc", 6)
                    SW(s, "xst0", 32 * 8)
                    SW(s, "xst1", 32 * 8)
                    s.copy(_c(bgb_o, 1024), pf2[:, :]).then_inc(sems["fbe"], 1)
                if k <= 2 * S - 1:
                    s_, h_, par = k // 2, k % 2, k % 2
                    SW(s, "rb%d" % par, 16 * (k // 2 + 1))
                    if k >= 2:
                        SW(s, "fmm", 2 * (k - 1))
                    s.activation(_b(xqT_o[par], 1024), _c(xr4_o[k % 4], 1024), AF.Square,
                                 accum_out=sm(C_S2))
                    s_st2(s.activation(_b(xqT_o[par], 1024), _c(xr4_o[k % 4], 1024),
                                       AF.Identity, accum_out=sm(C_S1)))
                    s.activation(sm(C_MS), sm(C_S2), AF.Identity, scale=1.0 / 1024.0)
                    s_st2(s.activation(sm(C_MEAN), sm(C_S1), AF.Identity,
                                       scale=-1.0 / 1024.0))
                    s_st2(s.activation(sm(C_MSQ), sm(C_MEAN), AF.Square))
                    s_st2(s.activation(sm(C_VAR), sm(C_MSQ), AF.Identity, scale=-1.0,
                                       bias=sm(C_MS)))
                    s_st2(s.activation(sm(C_VAR), sm(C_VAR), AF.Ln, bias=f32(EPS)))
                    s_st2(s.activation(sm(C_RINV), sm(C_VAR), AF.Exp, scale=-0.5))
                    s_st2(s.activation(sm(C_NM), sm(C_MEAN), AF.Identity,
                                       scale=sm(C_RINV)))
                    SW(s, "fmo", k)
                    s.activation(_c(tln_o, 1024), _c(xr4_o[k % 4], 1024), AF.Identity,
                                 bias=sm(C_NM), scale=sm(C_RINV)).then_inc(sems["fno"], 1)
                    if k >= 1:
                        SW(s, "ftp", 8 * k)
                        s.copy(_b(xqT_o[(k - 1) % 2], 1024),
                               pt[:, 0:1024]).then_inc(sems["fxq"], 1)
                    SW(s, "fam", k + 1)
                    s.activation(fs(F_L, par), fs(F_AMX, par), AF.Ln).then_inc(sems["fln"], 1)
                    SW(s, "fu", k + 1)
                    s_st2(s.copy(ftip(par), fs(F_U, par)))
                    s.copy(fs(F_TF, par), ftip(par)).then_inc(sems["fif"], 1)
                    SW(s, "fex", k + 1)
                    s.activation(scbp(par), fs(F_EB, par), AF.Exp, scale=LN2C)
                    s.activation(fs(F_RS, par), fs(F_EB, par), AF.Exp,
                                 scale=-LN2C).then_inc(sems["fsc"], 1)
                if k == 2 * S:
                    SW(s, "ftp", 8 * k)
                    s.copy(_b(xqT_o[(k - 1) % 2], 1024),
                           pt[:, 0:1024]).then_inc(sems["fxq"], 1)
                if k % 2 == 1:
                    sg = (k - 1) // 2
                    if sg >= 1:
                        SW(s, "fout", 2 * sg)
                        SW(s, "fbc", 6 * sg + 6)
                        s.copy(_c(bgb_o, 1024), pf2[:, :]).then_inc(sems["fbe"], 1)
                    sp = (k + 1) // 2
                    if sp <= S - 1:
                        SW(s, "fmo", k + 1)
                        SW(s, "fbc", 6 * sp + 2)
                        s.copy(cbS, pf2[:, :]).then_inc(sems["fbe"], 1)
                        SW(s, "fbc", 6 * sp + 4)
                        s.copy(sbS, pf2[:, :]).then_inc(sems["fbe"], 1)

        @blk.vector
        def _(v):
            vtn = [7]

            def stats_v(u, h):
                SW(v, "sq", 2 * u + h + 1)
                v.tensor_scalar(sm(C_MS + h), sm(C_S2 + h), 1.0 / 1024.0, None, ALU.mult)
                v.tensor_scalar(sm(C_MEAN + h), sm(C_S1 + h), 1.0 / 1024.0, None,
                                ALU.mult)
                v.scalar_tensor_tensor(sm(C_MSQ + h), sm(C_S1 + h), 1.0 / 1048576.0,
                                       sm(C_S1 + h), ALU.mult, ALU.mult).then_inc(sems["vt"], 1)
                vtn[0] += 1
                SW(v, "vt", vtn[0])
                v.tensor_sub(sm(C_VAR + h), sm(C_MS + h), sm(C_MSQ + h)).then_inc(sems["stat"], 1)

            def nm_v(u, h):
                SW(v, "rinv", 2 * u + h + 1)
                v.scalar_tensor_tensor(sm(C_NM + h), sm(C_MEAN + h), -1.0,
                                       sm(C_RINV + h), ALU.mult, ALU.mult).then_inc(sems["nm"], 1)

            for h in (0, 1):
                stats_v(0, h)
            for h in (0, 1):
                nm_v(0, h)

            ctmp = pf2[:, 0:256]

            def hs_v(st, jj):
                SW(v, "tphs", 16 * (st - 1) + 8 + jj + 1)
                v.tensor_scalar(_b(hs_o + 256 * jj, 256),
                                hT(st % 2)[:, 256 * jj:256 * (jj + 1)],
                                modT[:, 16 * (16 + jj) + st:16 * (16 + jj) + st + 1],
                                None, ALU.mult).then_inc(sems["hs"], 1)

            for st in range(S):
                for j in range(8):
                    if st >= 1 and j == 4:
                        u = st - 1
                        SW(v, "tphs", 16 * u + 16)
                        v.tensor_tensor(_c(xin_o[u % 3] + 1024, 1024),
                                        _c(xin_o[u % 3] + 1024, 1024),
                                        pt[:, 1024:2048], ALU.add).then_inc(sems["xn"], 1)
                    if st <= S - 2:
                        if j == 1:
                            stats_v(st + 1, 0)
                        elif j == 2:
                            stats_v(st + 1, 1)
                            nm_v(st + 1, 0)
                        elif j == 3:
                            nm_v(st + 1, 1)
                    SW(v, "gev", 32 * st + 4 * j + 4)
                    ig = gst_sl(0, j)
                    fg = gst_sl(1, j)
                    og = gst_sl(2, j)
                    cc = gst_sl(3, j)
                    cT_j = _c(cT_o + 256 * j, 256)
                    if st >= 1:
                        v.tensor_tensor(ctmp, fg, cT_j, ALU.mult)
                        v.tensor_tensor(cT_j, ig, cc, ALU.mult).then_inc(sems["vt"], 1)
                        vtn[0] += 1
                        SW(v, "vt", vtn[0])
                        v.tensor_tensor(cT_j, cT_j, ctmp, ALU.add).then_inc(sems["cupd"], 1)
                    else:
                        v.tensor_tensor(cT_j, ig, cc, ALU.mult).then_inc(sems["cupd"], 1)
                    SW(v, "tanh", 8 * st + j + 1)
                    v.tensor_tensor(hT(st % 2)[:, 256 * j:256 * (j + 1)], og,
                                    _b(ctt_o + 256 * (j % 2), 256), ALU.mult
                                    ).then_inc(sems["cv"], 1)
                    if j >= 3:
                        hs_v(st, j - 3)
                    if st >= 1 and j == 2:
                        u = st - 1
                        SW(v, "tphs", 16 * u + 8)
                        v.tensor_tensor(_c(xin_o[u % 3], 1024),
                                        _c(xin_o[u % 3], 1024),
                                        pt[:, 1024:2048], ALU.add).then_inc(sems["xn"], 1)
                v.engine_nop().then_inc(sems["vt"], 1)
                vtn[0] += 1
                SW(v, "vt", vtn[0])
                for jj in (5, 6, 7):
                    hs_v(st, jj)

            u = S - 1
            for h in (0, 1):
                SW(v, "tphs", 16 * u + 8 * (h + 1))
                v.tensor_tensor(_c(xin_o[u % 3] + 1024 * h, 1024),
                                _c(xin_o[u % 3] + 1024 * h, 1024),
                                pt[:, 1024:2048], ALU.add).then_inc(sems["xn"], 1)

            def eb_chain(par, dref, outsem):
                v.tensor_tensor(fs(F_CM, par), fs(F_U, par), fs(F_TF, par),
                                ALU.is_lt).then_inc(sems["vt"], 1)
                vtn[0] += 1
                SW(v, "vt", vtn[0])
                v.tensor_tensor(fs(F_EB, par), fs(F_TF, par), fs(F_CM, par),
                                ALU.subtract).then_inc(sems["vt"], 1)
                vtn[0] += 1
                v.tensor_scalar(fs(F_CM, par), fs(F_AMAX, par), 1e-38, None,
                                ALU.is_ge).then_inc(sems["vt"], 1)
                vtn[0] += 1
                SW(v, "vt", vtn[0])
                v.tensor_scalar(fs(F_EB, par), fs(F_EB, par), -160.0, None,
                                ALU.add).then_inc(sems["vt"], 1)
                vtn[0] += 1
                SW(v, "vt", vtn[0])
                v.tensor_tensor(fs(F_EB, par), fs(F_EB, par), fs(F_CM, par),
                                ALU.mult).then_inc(sems["vt"], 1)
                vtn[0] += 1
                SW(v, "vt", vtn[0])
                if dref[0] == "s":
                    v.tensor_scalar(fs(F_EB, par), fs(F_EB, par), dref[1], None,
                                    ALU.add).then_inc(sems[outsem], 1)
                else:
                    v.tensor_tensor(fs(F_EB, par), fs(F_EB, par), dref[1],
                                    ALU.add).then_inc(sems[outsem], 1)


            def blockchain_head(src, par):
                v.tensor_reduce(
                    fs(F_AMAX, par).rearrange("p (b o) -> p b o", o=1),
                    v3(src), AXX, ALU.max, apply_absolute_value=True).then_inc(sems["vt"], 1)
                vtn[0] += 1
                SW(v, "vt", vtn[0])
                v.tensor_scalar(fs(F_AMX, par), fs(F_AMAX, par), 1e-30, None, ALU.max)

            for oi in range(8):
                par = oi % 2
                SW(v, "wqd%d" % par, 16 * (oi // 2 + 1))
                if oi >= 2:
                    SW(v, "wql", oi - 1)
                wf_ap = _b(wf_o[par], 1024)
                blockchain_head(wf_ap, par)
                v.engine_nop().then_inc(sems["wqa"], 1)
                SW(v, "wql", oi + 1)
                v.tensor_scalar(fs(F_U, par), fs(F_L, par), 1.0 / LN2C, 158.0,
                                ALU.mult, ALU.add).then_inc(sems["wqu"], 1)
                SW(v, "wqi", oi + 1)
                eb_chain(par, ("t", dwh[:, 32 * oi:32 * (oi + 1)]), "wqx")
                SW(v, "wqc", oi + 1)
                SW(v, "wqt", 8 * oi)
                stair_ops(v, wf_ap, _b(xq_o[0], 1024), _b(tA_o, 1024), _b(tB_o, 1024),
                          _b(tC_o, 1024), _b(tD2_o, 1024), par, 0, 32)
                v.engine_nop().then_inc(sems["wq"], 1)
                SW(v, "wqt", 8 * (oi + 1))
                for j in range(8):
                    v.tensor_copy(_b(wqT_o + 1024 * j + 128 * oi, 128),
                                  pt[:, 1024 + 128 * j:1024 + 128 * (j + 1)]
                                  ).then_inc(sems["wqe"], 1)

            for k in range(2 * S + 2):
                if 1 <= k <= 2 * S:
                    j = k - 1
                    par = j % 2
                    SW(v, "fsc", j + 1)
                    SW(v, "ftp", 8 * (j - 1))
                    if j == 0:
                        SW(v, "wqt", 64)
                    stair_ops(v, _c(h2_o, 1024), _b(xq_o[par], 1024), _b(tA_o, 1024),
                              _b(tB_o, 1024), _b(tC_o, 1024), _b(tD_o, 1024), par, 0, 32)
                    v.engine_nop().then_inc(sems["fq"], 1)
                if k >= 2:
                    j = k - 2
                    par = j % 2
                    SW(v, "fbe", 3 * (j // 2) + 3)
                    SW(v, "fmm", 2 * (j + 1))
                    gf = _b(wf_o[1], 1024)
                    v.tensor_tensor(gf, pf[:, 1024 * par:1024 * (par + 1)],
                                    _c(bgb_o, 1024), ALU.mult)
                    v.tensor_tensor(_c(xr4_o[j % 4], 1024), gf, _c(xr4_o[j % 4], 1024),
                                    ALU.add).then_inc(sems["fout"], 1)
                if k <= 2 * S - 1:
                    s_, h_, par = k // 2, k % 2, k % 2
                    SW(v, "fno", k + 1)
                    SW(v, "fbe", 2 if s_ == 0 else 3 * s_ + 2)
                    h2 = _c(h2_o, 1024)
                    v.tensor_tensor(h2, _c(tln_o, 1024), cbS, ALU.mult)
                    v.tensor_tensor(h2, h2, sbS, ALU.add).then_inc(sems["fmo"], 1)
                    if k >= 2:
                        SW(v, "fln", k - 1)
                    else:
                        SW(v, "wql", 8)
                    blockchain_head(h2, par)
                    v.engine_nop().then_inc(sems["fam"], 1)
                    SW(v, "fln", k + 1)
                    v.tensor_scalar(fs(F_U, par), fs(F_L, par), 1.0 / LN2C, 158.0,
                                    ALU.mult, ALU.add).then_inc(sems["fu"], 1)
                    SW(v, "fif", k + 1)
                    eb_chain(par, ("s", dhalf[:, :]), "fex")

        @blk.sync
        def _(sync):
            for uu in (0, 1, 2):
                for h in (0, 1):
                    sync.dma_start(out=_c(xin_o[uu % 3] + 1024 * h, 1024),
                                   in_=x_d[uu, 128 * h:128 * (h + 1), :]
                                   ).then_inc(sems["xi%d%d" % (uu % 3, h)], 16)
            for st in range(S):
                if st >= 1:
                    u = st - 1
                    for h in (0, 1):
                        SW(sync, "xn", 2 * u + h + 1)
                        sync.dma_start(out=xnew_d[u, 128 * h:128 * (h + 1), :],
                                       in_=_c(xin_o[u % 3] + 1024 * h, 1024)
                                       ).then_inc(sems["xst%d" % (u % 2)], 16)
                uu = st + 2
                if 3 <= uu <= S - 1:
                    SW(sync, "xst%d" % ((uu - 3) % 2), 32 * ((uu - 3) // 2 + 1))
                    for h in (0, 1):
                        sync.dma_start(out=_c(xin_o[uu % 3] + 1024 * h, 1024),
                                       in_=x_d[uu, 128 * h:128 * (h + 1), :]
                                       ).then_inc(sems["xi%d%d" % (uu % 3, h)], 16)
            u = S - 1
            for h in (0, 1):
                SW(sync, "xn", 2 * u + h + 1)
                sync.dma_start(out=xnew_d[u, 128 * h:128 * (h + 1), :],
                               in_=_c(xin_o[u % 3] + 1024 * h, 1024)
                               ).then_inc(sems["xst%d" % (u % 2)], 16)

            SW(sync, "xst0", 32 * 8)
            SW(sync, "xst1", 32 * 8)
            for i in range(4):
                sync.dma_start(out=_c(xr4_o[i % 4], 1024),
                               in_=xnew_d[i // 2, 128 * (i % 2):128 * (i % 2 + 1), :]
                               ).then_inc(sems["rb%d" % (i % 2)], 16)
            for i in range(2 * S):
                s_, h_ = i // 2, i % 2
                SW(sync, "fout", i + 1)
                sync.dma_start(out=out_d[s_, 128 * h_:128 * (h_ + 1), :],
                               in_=_c(xr4_o[i % 4], 1024)).then_inc(sems["ost%d" % (i % 2)], 16)
                if i + 4 < 2 * S:
                    jj = i + 4
                    SW(sync, "ost%d" % (i % 2), 16 * (i // 2 + 1))
                    sync.dma_start(out=_c(xr4_o[jj % 4], 1024),
                                   in_=xnew_d[jj // 2, 128 * (jj % 2):128 * (jj % 2 + 1), :]
                                   ).then_inc(sems["rb%d" % (jj % 2)], 16)
            sync.wait_ge(sems["ost0"], 16 * S)
            sync.wait_ge(sems["ost1"], 16 * S)
    return nc


def kernel(L0_attn_input_seq, L0_attn_w, L0_attn_u, L0_attn_bias,
           L0_ffn_weight, L0_ffn_bias, L0_ffn_delta_w, L0_ffn_delta_act,
           L0_c, L0_wmod):
    x = np.ascontiguousarray(np.asarray(L0_attn_input_seq, dtype=np.float32))
    w = np.ascontiguousarray(np.asarray(L0_attn_w, dtype=np.float32))
    u = np.ascontiguousarray(np.asarray(L0_attn_u, dtype=np.float32))
    bias = np.ascontiguousarray(np.asarray(L0_attn_bias, dtype=np.float32))
    wffn = np.ascontiguousarray(np.asarray(L0_ffn_weight, dtype=np.float32))
    bffn = np.ascontiguousarray(np.asarray(L0_ffn_bias, dtype=np.float32)).reshape(1, -1)
    dw = np.ascontiguousarray(np.asarray(L0_ffn_delta_w, dtype=np.float32))
    dact = np.ascontiguousarray(np.asarray(L0_ffn_delta_act, dtype=np.float32)).reshape(1, 1)
    cmod = np.ascontiguousarray(np.asarray(L0_c, dtype=np.float32))
    wmod = np.ascontiguousarray(np.asarray(L0_wmod, dtype=np.float32))

    identb = np.eye(128, dtype=ml_dtypes.bfloat16)
    identf = np.eye(128, dtype=np.float32)
    ones1 = np.ones((1, 128), dtype=ml_dtypes.bfloat16)

    nc = build_program()
    in_maps = []
    for c in range(NCORES):
        xs = np.ascontiguousarray(x[:, B * c:B * (c + 1), :])
        in_maps.append(dict(x=xs, w_attn=w, u_attn=u, bias_attn=bias,
                            wmod=wmod, cmod=cmod, wffn=wffn, bffn=bffn,
                            dw=dw, dact=dact, identb=identb, identf=identf,
                            ones1=ones1))
    res = run_bass_kernel_spmd(nc, in_maps, core_ids=list(range(NCORES)))
    return np.concatenate([r["out"] for r in res.results], axis=1)


# version 16 (final)
# speedup vs baseline: 1.8030x; 1.0100x over previous
import numpy as np
import ml_dtypes
import concourse.bass as bass
import concourse.mybir as mybir
from concourse.bass_utils import run_bass_kernel_spmd

F32 = mybir.dt.float32
BF16 = mybir.dt.bfloat16
I32 = mybir.dt.int32
AF = mybir.ActivationFunctionType
ALU = mybir.AluOpType
AXX = mybir.AxisListType.X

NCORES = 8
S = 16
B = 256
D = 1024
LN2C = float(np.log(2.0))
EPS = 1e-5
f32 = np.float32


def build_program():
    nc = bass.Bass("TRN2", target_bir_lowering=False, debug=False,
                   detect_race_conditions=False)

    x_d = nc.dram_tensor("x", [S, B, D], F32, kind="ExternalInput")
    w_d = nc.dram_tensor("w_attn", [D, 4 * D], F32, kind="ExternalInput")
    u_d = nc.dram_tensor("u_attn", [D, 4 * D], F32, kind="ExternalInput")
    bias_d = nc.dram_tensor("bias_attn", [4 * D], F32, kind="ExternalInput")
    wmod_d = nc.dram_tensor("wmod", [D, 6 * D], F32, kind="ExternalInput")
    cmod_d = nc.dram_tensor("cmod", [S, D], F32, kind="ExternalInput")
    wffn_d = nc.dram_tensor("wffn", [D, D], F32, kind="ExternalInput")
    bffn_d = nc.dram_tensor("bffn", [1, D], F32, kind="ExternalInput")
    dw_d = nc.dram_tensor("dw", [D, 32], F32, kind="ExternalInput")
    dact_d = nc.dram_tensor("dact", [1, 1], F32, kind="ExternalInput")
    identb_d = nc.dram_tensor("identb", [128, 128], BF16, kind="ExternalInput")
    identf_d = nc.dram_tensor("identf", [128, 128], F32, kind="ExternalInput")
    ones_d = nc.dram_tensor("ones1", [1, 128], BF16, kind="ExternalInput")
    xnew_d = nc.dram_tensor("xnew_h", [S, B, D], F32, kind="Internal")
    out_d = nc.dram_tensor("out", [S, B, D], F32, kind="ExternalOutput")

    arena_w = nc.alloc_sbuf_tensor("arena_w", [128, 16 * 4096], BF16)

    def wu(k):
        return arena_w[:, 4096 * k:4096 * (k + 1)]

    def wmod_t(kk):
        off = 4096 * 4 + 6144 * kk
        return arena_w[:, off:off + 6144]

    AB = 17408
    arena_B = nc.alloc_sbuf_tensor("arena_B", [128, AB], BF16)

    def _b(off, n):
        return arena_B[:, off:off + n]
    gst_o = 0
    hnT_o = [4096, 6144]
    hT_o = [8192, 10240]
    ctt_o = 12288
    hs_o = 12800
    xln_o = 14848
    wqT_o = 0
    wf_o = [8192, 9216]
    xq_o = [10240, 11264]
    xqT_o = [12288, 13312]
    tA_o, tB_o, tC_o = 14336, 15360, 16384
    tD_o = wf_o[0]
    tD2_o = xq_o[1]
    csil_o = 0

    AC = 8192
    arena_C = nc.alloc_sbuf_tensor("arena_C", [128, AC], F32)

    def _c(off, n):
        return arena_C[:, off:off + n]
    cT_o = 0
    xin_o = [2048, 4096, 6144]
    xr_o = [0, 1024]
    tln_o = 2048
    h2_o = 3072
    outst_o = [4096, 5120]
    bgb_o = 6144
    xr4_o = [0, 1024, 4096, 5120]
    cin_o = 0
    modn_o = 1024

    cbS = arena_w[:, 0:1024]
    sbS = arena_w[:, 1024:2048]
    identb = nc.alloc_sbuf_tensor("identb_s", [128, 128], BF16)
    identf = nc.alloc_sbuf_tensor("identf_s", [128, 128], F32)
    ones1 = nc.alloc_sbuf_tensor("ones1_s", [1, 128], BF16)
    siluCT = nc.alloc_sbuf_tensor("siluCT", [128, 128], BF16)
    modT = nc.alloc_sbuf_tensor("modT", [128, 384], F32)
    modnatb = nc.alloc_sbuf_tensor("modnatb", [16, 3072], BF16)
    biasT = nc.alloc_sbuf_tensor("biasT", [128, 32], F32)
    fbias = nc.alloc_sbuf_tensor("fbias", [1, 1024], BF16)
    dwh = nc.alloc_sbuf_tensor("dwh", [128, 256], BF16)
    dhalf = nc.alloc_sbuf_tensor("dhalf", [128, 1], F32)
    dactb = nc.alloc_sbuf_tensor("dactb", [1, 1], BF16)
    smalls = nc.alloc_sbuf_tensor("smalls", [128, 24], F32)
    fsm = nc.alloc_sbuf_tensor("fsm", [128, 448], F32)
    fti = nc.alloc_sbuf_tensor("fti", [128, 64], I32)
    scb = nc.alloc_sbuf_tensor("scb", [128, 64], BF16)

    tmpm = arena_C[:, 7168:7296]
    tmpm2 = arena_C[:, 7296:7424]
    dwt = arena_C[:, 7424:7680]
    biasN = arena_C[:32, 7680:7808]

    C_S1, C_S2, C_MEAN, C_MS, C_MSQ, C_VAR, C_RINV, C_NM = 0, 2, 4, 6, 8, 10, 12, 14
    C_DACT, C_T4D = 16, 17

    def sm(col):
        return smalls[:, col:col + 1]

    F_AMAX, F_AMX, F_L, F_U, F_TF, F_CM, F_EB = 0, 32, 64, 96, 128, 160, 192
    F_RS = F_U

    def fs(c0, par=0):
        c = 224 * par + c0
        return fsm[:, c:c + 32]

    def ftip(par):
        return fti[:, 32 * par:32 * (par + 1)]

    def scbp(par):
        return scb[:, 32 * par:32 * (par + 1)]

    pf = nc.alloc_psum_tensor("pf", [128, 2048], F32)
    pf2 = nc.alloc_psum_tensor("pf2", [128, 1024], F32)
    pt = nc.alloc_psum_tensor("pt", [128, 2048], BF16)

    def gslot(p):
        i = p % 8
        c = 512 * (i % 4) + 256 * (i // 4)
        return pf[:, c:c + 256]

    sems = {}
    for name in ["pd", "wmod", "ww", "wu2", "wsm", "pr", "pr2", "xin00", "xin01", "xin10", "xin11",
                 "sq", "stat", "rinv", "nm",
                 "ln", "tphn", "hn", "gmm", "gev", "cupd", "tanh", "cv", "hs",
                 "tphs", "xn", "xst0", "xst1",
                 "wqd0", "wqd1", "wqa", "wql", "wqu", "wqi", "wqx", "wqc", "wq", "wqt",
                 "wqe", "rb0", "rb1", "fsq", "fst", "fri", "fnm", "fno", "fbc",
                 "fbgbe", "fmo", "fam", "fln", "fu", "fif", "fex", "fsc",
                 "fq", "ftp", "fxq", "fmm", "fout", "ost0", "ost1", "vt", "st2",
                 "fbe", "hn1",
                 "xi00", "xi01", "xi10", "xi11", "xi20", "xi21"]:
        sems[name] = nc.alloc_semaphore("s_" + name)

    def SW(eng, name, val):
        if val > 0:
            eng.wait_ge(sems[name], int(val))

    wu_waited = set()

    def wu_wait(eng, k):
        key = "ww" if k < 8 else "wu2"
        if key not in wu_waited:
            wu_waited.add(key)
            SW(eng, key, 128)

    def rep32(ap):
        return ap.rearrange("p (b o) -> p b o", o=1).broadcast_to((128, 32, 32))

    def v3(ap):
        return ap.rearrange("p (b i) -> p b i", i=32)

    with nc.Block() as blk:
        @blk.sync
        def _(sync):
            sync.dma_start(out=identb[:, :], in_=identb_d[:, :]).then_inc(sems["pd"], 16)
            sync.dma_start(out=identf[:, :], in_=identf_d[:, :]).then_inc(sems["pd"], 16)
            sync.dma_start(out=ones1[:, :], in_=ones_d[:, :]).then_inc(sems["pd"], 16)
            sync.dma_start(out=_c(cin_o, 1024)[:16, :], in_=cmod_d[:, :]).then_inc(sems["pd"], 16)
            sync.dma_start(out=biasN[:, :],
                           in_=bias_d[:].rearrange("(m p) -> m p", p=128)).then_inc(sems["pd"], 16)
            sync.dma_start(out=dwt[:, :].rearrange("p (a c) -> p a c", c=32),
                           in_=dw_d[:, :].rearrange("(a p) c -> p a c", p=128)).then_inc(sems["pd"], 16)

        @blk.gpsimd
        def _(g):
            for kk in range(8):
                g.dma_start(out=wmod_t(kk), in_=wmod_d[128 * kk:128 * (kk + 1), :]).then_inc(sems["wmod"], 16)
            for k in range(4):
                g.dma_start(out=wu(k), in_=w_d[128 * k:128 * (k + 1), :]).then_inc(sems["ww"], 16)
            g.dma_start(out=fbias[:, :], in_=bffn_d[:, :]).then_inc(sems["wsm"], 16)
            g.dma_start(out=dactb[:, :], in_=dact_d[:, :]).then_inc(sems["wsm"], 16)

    with nc.Block() as blk:
        @blk.scalar
        def _(s):
            SW(s, "pd", 96)
            s.activation(_b(csil_o, 1024)[:16, :], _c(cin_o, 1024)[:16, :],
                         AF.Silu).then_inc(sems["pr"], 1)
            SW(s, "pd", 96)
            s.activation(dwh[:, :], dwt[:, :], AF.Tanh, scale=4.0).then_inc(sems["pr"], 1)

        @blk.tensor
        def _(t):
            SW(t, "pr", 1)
            SW(t, "pd", 96)
            for kk in range(8):
                t.transpose(pt[:, 128 * kk:128 * kk + 16],
                            _b(csil_o, 1024)[:16, 128 * kk:128 * (kk + 1)],
                            identb[:16, :16]).then_inc(sems["pr2"], 1)
            SW(t, "pd", 96)
            t.transpose(pf[:, 0:32], biasN[:, :], identf[:32, :32]).then_inc(sems["pr2"], 1)
            SW(t, "wsm", 32)
            t.matmul(pf[:, 32:33], ones1[:, :], dactb[:, :],
                     start=True, stop=True).then_inc(sems["pr2"], 1)

        @blk.vector
        def _(v):
            SW(v, "pr2", 8)
            v.tensor_copy(
                siluCT[:, :].rearrange("p (j c) -> p j c", c=16),
                pt[:, :1024].rearrange("p (j c) -> p j c", c=128)[:, :, :16])
            SW(v, "pr2", 10)
            v.tensor_copy(biasT[:, :], pf[:, 0:32])
            v.tensor_copy(sm(C_DACT), pf[:, 32:33]).then_inc(sems["pr"], 1)
            v.tensor_scalar(dwh[:, :], dwh[:, :], 0.5, None, ALU.mult)
            SW(v, "pr", 4)
            v.tensor_scalar(dhalf[:, :], sm(C_T4D), 0.5, None, ALU.mult)

        @blk.scalar
        def _(s2_):
            SW(s2_, "pr", 3)
            s2_.activation(sm(C_T4D), sm(C_DACT), AF.Tanh,
                           scale=4.0).then_inc(sems["pr"], 1)


    with nc.Block(no_gpsimd_drain=True) as blk:
        @blk.tensor
        def _(t):
            SW(t, "wmod", 128)
            for c in range(12):
                reg = pf[:16, 512 * (c % 2):512 * (c % 2) + 512]
                if c >= 2:
                    SW(t, "pr", c + 3)
                for kk in range(8):
                    mm = t.matmul(reg, siluCT[:, 16 * kk:16 * (kk + 1)],
                                  wmod_t(kk)[:, 512 * c:512 * (c + 1)],
                                  start=(kk == 0), stop=(kk == 7))
                    if kk == 7:
                        mm.then_inc(sems["pr2"], 1)

        @blk.scalar
        def _(s):
            for c in range(12):
                SW(s, "pr2", 11 + c)
                s.copy(_c(modn_o, 6144)[:16, 512 * c:512 * (c + 1)],
                       pf[:16, 512 * (c % 2):512 * (c % 2) + 512]).then_inc(sems["pr"], 1)

    with nc.Block(no_gpsimd_drain=True) as blk:
        @blk.tensor
        def _(t):
            SW(t, "pr", 16)
            for jj in range(24):
                tp = t.transpose(pf[:, 1024 + 16 * jj:1024 + 16 * (jj + 1)],
                                 _c(modn_o, 6144)[:16, 128 * jj:128 * (jj + 1)],
                                 identf[:16, :16])
                if jj == 23:
                    tp.then_inc(sems["pr2"], 1)

        @blk.vector
        def _(v):
            SW(v, "pr2", 23)
            v.tensor_copy(
                modT[:, :].rearrange("p (j c) -> p j c", c=16),
                pf[:, 1024:1024 + 384].rearrange("p (j c) -> p j c", c=16))
            v.tensor_scalar(modT[:, 128:256], modT[:, 128:256], 1.0, None, ALU.add)
            ga = modT[:, 256:384]
            v.tensor_scalar(tmpm[:, :], ga, 0.0, None, ALU.is_ge)
            v.tensor_scalar(tmpm2[:, :], ga, 0.25, None, ALU.max)
            v.tensor_scalar(ga, ga, -0.25, None, ALU.min).then_inc(sems["vt"], 1)
            v.wait_ge(sems["vt"], 1)
            v.tensor_sub(tmpm2[:, :], tmpm2[:, :], ga).then_inc(sems["vt"], 1)
            v.wait_ge(sems["vt"], 2)
            v.tensor_mul(tmpm[:, :], tmpm[:, :], tmpm2[:, :]).then_inc(sems["vt"], 1)
            v.wait_ge(sems["vt"], 3)
            v.tensor_add(ga, ga, tmpm[:, :])
            nat = _c(modn_o, 6144)[:16, :]
            v.tensor_scalar(nat[:, 4096:5120], nat[:, 4096:5120], 1.0, None, ALU.add)
            gb = nat[:, 5120:6144]
            m16 = nat[:, 0:1024]
            a16 = nat[:, 1024:2048]
            v.tensor_scalar(m16, gb, 0.0, None, ALU.is_ge)
            v.tensor_scalar(a16, gb, 0.25, None, ALU.max)
            v.tensor_scalar(gb, gb, -0.25, None, ALU.min).then_inc(sems["vt"], 1)
            v.wait_ge(sems["vt"], 4)
            v.tensor_sub(a16, a16, gb).then_inc(sems["vt"], 1)
            v.wait_ge(sems["vt"], 5)
            v.tensor_mul(m16, m16, a16).then_inc(sems["vt"], 1)
            v.wait_ge(sems["vt"], 6)
            v.tensor_add(gb, gb, m16).then_inc(sems["vt"], 1)
            v.wait_ge(sems["vt"], 7)
            v.tensor_copy(modnatb[:, :], nat[:, 3072:6144])

    def gst_sl(role, j):
        return _b(gst_o + (role * 2 + (j % 2)) * 256, 256)

    def hnT(par):
        return _b(hnT_o[par], 2048)

    def hT(par):
        return _b(hT_o[par], 2048)

    with nc.Block() as blk:
        @blk.gpsimd
        def _(g):
            for k in range(4, 8):
                g.dma_start(out=wu(k), in_=w_d[128 * k:128 * (k + 1), :]).then_inc(sems["ww"], 16)
            for k in range(8):
                g.dma_start(out=wu(8 + k), in_=u_d[128 * k:128 * (k + 1), :]).then_inc(sems["wu2"], 16)
            SW(g, "gmm", 32 * S)
            for oi in range(8):
                SW(g, "wq", oi - 1)
                g.dma_start(out=_b(wf_o[oi % 2], 1024),
                            in_=wffn_d[128 * oi:128 * (oi + 1), :]).then_inc(sems["wqd%d" % (oi % 2)], 16)

        @blk.tensor
        def _(t):
            def hn_burst(u, h):
                SW(t, "ln", 2 * u + h + 1)
                if h == 0:
                    SW(t, "hn", 8 * u)
                else:
                    SW(t, "xn", 2 * u - 2)
                    SW(t, "hn1", 8 * u)
                base = 0 if h == 0 else 1024
                for j in range(8):
                    t.transpose(pt[:, base + 128 * j:base + 128 * (j + 1)],
                                _b(xln_o + 1024 * h, 1024)[:, 128 * j:128 * (j + 1)],
                                identb[:, :]).then_inc(sems["tphn"], 1)

            def hs_burst(u, h):
                SW(t, "xn", 2 * u + h)
                SW(t, "hn1", min(8 * (u + 1) + 8, 8 * S))
                for j in range(8):
                    SW(t, "hs", 8 * u + j + 1)
                    t.transpose(pt[:, 1024 + 128 * j:1024 + 128 * (j + 1)],
                                _b(hs_o, 2048)[:, 256 * j + 128 * h:256 * j + 128 * (h + 1)],
                                identb[:, :]).then_inc(sems["tphs"], 1)

            hn_burst(0, 0)
            hn_burst(0, 1)

            for st in range(S):
                G0 = [0, 8, 16, 24]
                if st == 0:
                    for idx, m in enumerate(G0):
                        SW(t, "gev", 32 * st + idx - 3)
                        for k in range(8):
                            wu_wait(t, k)
                            if idx == 0:
                                SW(t, "hn", 8 * st + k + 1)
                                SW(t, "hn1", 8 * st + k + 1)
                            mm = t.matmul(gslot(idx), wu(k)[:, 128 * m:128 * (m + 1)],
                                          hnT(0)[:, 256 * k:256 * (k + 1)],
                                          start=(k == 0), stop=(k == 7))
                            if k == 7:
                                mm.then_inc(sems["gmm"], 1)
                else:
                    for idx, m in enumerate(G0):
                        SW(t, "gev", 32 * st + idx - 3)
                        for k in range(8):
                            wu_wait(t, k)
                            if idx == 0:
                                SW(t, "hn", 8 * st + k + 1)
                                SW(t, "hn1", 8 * st + k + 1)
                            t.matmul(gslot(idx), wu(k)[:, 128 * m:128 * (m + 1)],
                                     hnT(st % 2)[:, 256 * k:256 * (k + 1)],
                                     start=(k == 0), stop=False)
                    for idx, m in enumerate(G0):
                        for k in range(8, 16):
                            wu_wait(t, k)
                            if idx == 0:
                                SW(t, "cv", 8 * (st - 1) + (k - 8) + 1)
                            mm = t.matmul(gslot(idx), wu(k)[:, 128 * m:128 * (m + 1)],
                                          hT((st - 1) % 2)[:, 256 * (k - 8):256 * (k - 7)],
                                          start=False, stop=(k == 15))
                            if k == 15:
                                mm.then_inc(sems["gmm"], 1)

                for j in range(1, 8):
                    for idx, m in enumerate([j, j + 8, j + 16, j + 24]):
                        p = 4 * j + idx
                        SW(t, "gev", 32 * st + p - 3)
                        ks = range(8) if st == 0 else range(16)
                        for k in ks:
                            rhs = (hnT(st % 2)[:, 256 * k:256 * (k + 1)] if k < 8 else
                                   hT((st - 1) % 2)[:, 256 * (k - 8):256 * (k - 7)])
                            mm = t.matmul(gslot(p), wu(k)[:, 128 * m:128 * (m + 1)],
                                          rhs, start=(k == 0),
                                          stop=(k == (7 if st == 0 else 15)))
                            if k == (7 if st == 0 else 15):
                                mm.then_inc(sems["gmm"], 1)
                    if j == 1 and st >= 1:
                        hs_burst(st - 1, 0)
                    if j == 3 and st >= 1:
                        hs_burst(st - 1, 1)
                    if st <= S - 2 and j == 5:
                        hn_burst(st + 1, 0)
                        hn_burst(st + 1, 1)

            hs_burst(S - 1, 0)
            hs_burst(S - 1, 1)

            SW(t, "xn", 2 * S)
            SW(t, "cupd", 8 * S)
            for oi in range(8):
                SW(t, "wq", oi + 1)
                SW(t, "wqe", 8 * oi)
                for j in range(8):
                    t.transpose(pt[:, 1024 + 128 * j:1024 + 128 * (j + 1)],
                                _b(xq_o[0], 1024)[:, 128 * j:128 * (j + 1)],
                                identb[:, :]).then_inc(sems["wqt"], 1)

            def bcast(u_, src_c):
                for n in range(2):
                    t.matmul(pf2[:, 512 * n:512 * (n + 1)],
                             identb[:16, u_:u_ + 1].broadcast_to((16, 128)),
                             modnatb[:16, src_c + 512 * n:src_c + 512 * (n + 1)],
                             start=True, stop=True).then_inc(sems["fbc"], 1)

            def bc_trio(u_):
                SW(t, "fbe", 3 * u_)
                bcast(u_, 1024)
                SW(t, "fbe", 3 * u_ + 1)
                bcast(u_, 0)
                SW(t, "fbe", 3 * u_ + 2)
                bcast(u_, 2048)

            for k in range(2 * S + 1):
                if k == 0:
                    bc_trio(0)
                if k >= 1:
                    j = k - 1
                    par = j % 2
                    SW(t, "fq", k)
                    SW(t, "fxq", j)
                    for jj in range(8):
                        t.transpose(pt[:, 128 * jj:128 * (jj + 1)],
                                    _b(xq_o[par], 1024)[:, 128 * jj:128 * (jj + 1)],
                                    identb[:, :]).then_inc(sems["ftp"], 1)
                    SW(t, "fxq", k)
                    if j >= 2:
                        SW(t, "fout", j - 1)
                    for n in range(2):
                        for kk in range(8):
                            t.matmul(pf[:, 1024 * par + 512 * n:1024 * par + 512 * (n + 1)],
                                     _b(xqT_o[par], 1024)[:, 128 * kk:128 * (kk + 1)],
                                     _b(wqT_o + 1024 * kk, 1024)[:, 512 * n:512 * (n + 1)],
                                     start=(kk == 0), stop=False)
                        t.matmul(pf[:, 1024 * par + 512 * n:1024 * par + 512 * (n + 1)],
                                 ones1[:, :], fbias[:, 512 * n:512 * (n + 1)],
                                 start=False, stop=True).then_inc(sems["fmm"], 1)
                if k % 2 == 1 and (k + 1) // 2 <= S - 1:
                    bc_trio((k + 1) // 2)

        @blk.scalar
        def _(s):
            stn = [0]

            def stats_sq(u, h):
                SW(s, "xi%d%d" % (u % 3, h), 16 * (u // 3 + 1))
                SW(s, "tphn", 16 * (u - 1) + 16)
                if u >= 1:
                    SW(s, "stat", 2 * (u - 1) + h + 1)
                s.activation(_b(xln_o + 1024 * h, 1024),
                             _c(xin_o[u % 3] + 1024 * h, 1024), AF.Square,
                             accum_out=sm(C_S2 + h))
                s.activation(_b(xln_o + 1024 * h, 1024),
                             _c(xin_o[u % 3] + 1024 * h, 1024), AF.Identity,
                             accum_out=sm(C_S1 + h)).then_inc(sems["sq"], 1)

            def rinv_chain(u, h):
                SW(s, "stat", 2 * u + h + 1)
                s.activation(sm(C_VAR + h), sm(C_VAR + h), AF.Ln,
                             bias=f32(EPS)).then_inc(sems["st2"], 1)
                stn[0] += 1
                SW(s, "st2", stn[0])
                s.activation(sm(C_RINV + h), sm(C_VAR + h), AF.Exp,
                             scale=-0.5).then_inc(sems["rinv"], 1)
                stn[0] += 1
                s.nop().then_inc(sems["st2"], 1)

            def normalize(u, h):
                SW(s, "nm", 2 * u + h + 1)
                SW(s, "st2", stn[0])
                s.activation(_b(xln_o + 1024 * h, 1024),
                             _c(xin_o[u % 3] + 1024 * h, 1024), AF.Identity,
                             bias=sm(C_NM + h), scale=sm(C_RINV + h)).then_inc(sems["ln"], 1)

            def hn_evacs(u, h):
                SW(s, "tphn", 16 * u + 8 * h + 8)
                base = 0 if h == 0 else 1024
                for j in range(8):
                    s.activation(hnT(u % 2)[:, 256 * j + 128 * h:256 * j + 128 * (h + 1)],
                                 pt[:, base + 128 * j:base + 128 * (j + 1)], AF.Identity,
                                 bias=modT[:, 16 * j + u:16 * j + u + 1],
                                 scale=modT[:, 16 * (8 + j) + u:16 * (8 + j) + u + 1]
                                 ).then_inc(sems["hn1" if h else "hn"], 1)

            for h in (0, 1):
                stats_sq(0, h)
            for h in (0, 1):
                rinv_chain(0, h)
                normalize(0, h)
            hn_evacs(0, 0)
            hn_evacs(0, 1)

            for st in range(S):
                for j in range(8):
                    SW(s, "gmm", 32 * st + 4 * (j + 1))
                    SW(s, "cv", 8 * st + j - 1)
                    for role in range(4):
                        p = 4 * j + role
                        func = AF.Tanh if role == 3 else AF.Sigmoid
                        m = role * 8 + j
                        s.activation(gst_sl(role, j), gslot(p), func,
                                     bias=biasT[:, m:m + 1]).then_inc(sems["gev"], 1)
                    if j >= 1:
                        SW(s, "cupd", 8 * st + j)
                        s.activation(_b(ctt_o + 256 * ((j - 1) % 2), 256),
                                     _c(cT_o + 256 * (j - 1), 256), AF.Tanh
                                     ).then_inc(sems["tanh"], 1)
                    if st <= S - 2:
                        u = st + 1
                        if j == 0:
                            stats_sq(u, 0)
                        elif j == 1:
                            stats_sq(u, 1)
                        elif j == 2:
                            rinv_chain(u, 0)
                        elif j == 3:
                            rinv_chain(u, 1)
                        elif j == 4:
                            normalize(u, 0)
                            normalize(u, 1)
                        elif j == 5:
                            hn_evacs(u, 0)
                        elif j == 6:
                            hn_evacs(u, 1)
                SW(s, "cupd", 8 * st + 8)
                s.activation(_b(ctt_o + 256 * (7 % 2), 256), _c(cT_o + 256 * 7, 256),
                             AF.Tanh).then_inc(sems["tanh"], 1)

            for oi in range(8):
                par = oi % 2
                SW(s, "wqa", oi + 1)
                s.activation(fs(F_L, par), fs(F_AMX, par), AF.Ln).then_inc(sems["wql"], 1)
                SW(s, "wqu", oi + 1)
                s.copy(ftip(par), fs(F_U, par)).then_inc(sems["st2"], 1)
                stn[0] += 1
                SW(s, "st2", stn[0])
                s.copy(fs(F_TF, par), ftip(par)).then_inc(sems["wqi"], 1)
                SW(s, "wqx", oi + 1)
                s.activation(scbp(par), fs(F_EB, par), AF.Exp, scale=LN2C)
                s.activation(fs(F_RS, par), fs(F_EB, par), AF.Exp,
                             scale=-LN2C).then_inc(sems["wqc"], 1)

            def s_st2(op):
                op.then_inc(sems["st2"], 1)
                stn[0] += 1
                SW(s, "st2", stn[0])

            for k in range(2 * S + 1):
                if k == 0:
                    SW(s, "fbc", 2)
                    s.copy(cbS, pf2[:, :]).then_inc(sems["fbe"], 1)
                    SW(s, "fbc", 4)
                    s.copy(sbS, pf2[:, :]).then_inc(sems["fbe"], 1)
                    SW(s, "fbc", 6)
                    SW(s, "xst0", 32 * 8)
                    SW(s, "xst1", 32 * 8)
                    s.copy(_c(bgb_o, 1024), pf2[:, :]).then_inc(sems["fbe"], 1)
                if k <= 2 * S - 1:
                    s_, h_, par = k // 2, k % 2, k % 2
                    SW(s, "rb%d" % par, 16 * (k // 2 + 1))
                    if k >= 2:
                        SW(s, "fmm", 2 * (k - 1))
                    s.activation(_b(xqT_o[par], 1024), _c(xr4_o[k % 4], 1024), AF.Square,
                                 accum_out=sm(C_S2))
                    s_st2(s.activation(_b(xqT_o[par], 1024), _c(xr4_o[k % 4], 1024),
                                       AF.Identity, accum_out=sm(C_S1)))
                    s.activation(sm(C_MS), sm(C_S2), AF.Identity, scale=1.0 / 1024.0)
                    s_st2(s.activation(sm(C_MEAN), sm(C_S1), AF.Identity,
                                       scale=-1.0 / 1024.0))
                    s_st2(s.activation(sm(C_MSQ), sm(C_MEAN), AF.Square))
                    s_st2(s.activation(sm(C_VAR), sm(C_MSQ), AF.Identity, scale=-1.0,
                                       bias=sm(C_MS)))
                    s_st2(s.activation(sm(C_VAR), sm(C_VAR), AF.Ln, bias=f32(EPS)))
                    s_st2(s.activation(sm(C_RINV), sm(C_VAR), AF.Exp, scale=-0.5))
                    s_st2(s.activation(sm(C_NM), sm(C_MEAN), AF.Identity,
                                       scale=sm(C_RINV)))
                    SW(s, "fmo", k)
                    s.activation(_c(tln_o, 1024), _c(xr4_o[k % 4], 1024), AF.Identity,
                                 bias=sm(C_NM), scale=sm(C_RINV)).then_inc(sems["fno"], 1)
                    if k >= 1:
                        SW(s, "ftp", 8 * k)
                        s.copy(_b(xqT_o[(k - 1) % 2], 1024),
                               pt[:, 0:1024]).then_inc(sems["fxq"], 1)
                    SW(s, "fam", k + 1)
                    s.activation(fs(F_L, par), fs(F_AMX, par), AF.Ln).then_inc(sems["fln"], 1)
                    SW(s, "fu", k + 1)
                    s_st2(s.copy(ftip(par), fs(F_U, par)))
                    s.copy(fs(F_TF, par), ftip(par)).then_inc(sems["fif"], 1)
                    SW(s, "fex", k + 1)
                    s.activation(scbp(par), fs(F_EB, par), AF.Exp, scale=LN2C)
                    s.activation(fs(F_RS, par), fs(F_EB, par), AF.Exp,
                                 scale=-LN2C).then_inc(sems["fsc"], 1)
                if k == 2 * S:
                    SW(s, "ftp", 8 * k)
                    s.copy(_b(xqT_o[(k - 1) % 2], 1024),
                           pt[:, 0:1024]).then_inc(sems["fxq"], 1)
                if k % 2 == 1:
                    sg = (k - 1) // 2
                    if sg >= 1:
                        SW(s, "fout", 2 * sg)
                        SW(s, "fbc", 6 * sg + 6)
                        s.copy(_c(bgb_o, 1024), pf2[:, :]).then_inc(sems["fbe"], 1)
                    sp = (k + 1) // 2
                    if sp <= S - 1:
                        SW(s, "fmo", k + 1)
                        SW(s, "fbc", 6 * sp + 2)
                        s.copy(cbS, pf2[:, :]).then_inc(sems["fbe"], 1)
                        SW(s, "fbc", 6 * sp + 4)
                        s.copy(sbS, pf2[:, :]).then_inc(sems["fbe"], 1)

        @blk.vector
        def _(v):
            vtn = [7]

            def stats_v(u, h):
                SW(v, "sq", 2 * u + h + 1)
                v.tensor_scalar(sm(C_MS + h), sm(C_S2 + h), 1.0 / 1024.0, None, ALU.mult)
                v.tensor_scalar(sm(C_MEAN + h), sm(C_S1 + h), 1.0 / 1024.0, None,
                                ALU.mult)
                v.scalar_tensor_tensor(sm(C_MSQ + h), sm(C_S1 + h), 1.0 / 1048576.0,
                                       sm(C_S1 + h), ALU.mult, ALU.mult).then_inc(sems["vt"], 1)
                vtn[0] += 1
                SW(v, "vt", vtn[0])
                v.tensor_sub(sm(C_VAR + h), sm(C_MS + h), sm(C_MSQ + h)).then_inc(sems["stat"], 1)

            def nm_v(u, h):
                SW(v, "rinv", 2 * u + h + 1)
                v.scalar_tensor_tensor(sm(C_NM + h), sm(C_MEAN + h), -1.0,
                                       sm(C_RINV + h), ALU.mult, ALU.mult).then_inc(sems["nm"], 1)

            for h in (0, 1):
                stats_v(0, h)
            for h in (0, 1):
                nm_v(0, h)

            ctmp = pf2[:, 0:256]

            def hs_v(st, jj):
                SW(v, "tphs", 16 * (st - 1) + 8 + jj + 1)
                v.tensor_scalar(_b(hs_o + 256 * jj, 256),
                                hT(st % 2)[:, 256 * jj:256 * (jj + 1)],
                                modT[:, 16 * (16 + jj) + st:16 * (16 + jj) + st + 1],
                                None, ALU.mult).then_inc(sems["hs"], 1)

            for st in range(S):
                for j in range(8):
                    if st >= 1 and j == 4:
                        u = st - 1
                        SW(v, "tphs", 16 * u + 16)
                        v.tensor_tensor(_c(xin_o[u % 3] + 1024, 1024),
                                        _c(xin_o[u % 3] + 1024, 1024),
                                        pt[:, 1024:2048], ALU.add).then_inc(sems["xn"], 1)
                    if st <= S - 2:
                        if j == 1:
                            stats_v(st + 1, 0)
                        elif j == 2:
                            stats_v(st + 1, 1)
                            nm_v(st + 1, 0)
                        elif j == 3:
                            nm_v(st + 1, 1)
                    SW(v, "gev", 32 * st + 4 * j + 4)
                    ig = gst_sl(0, j)
                    fg = gst_sl(1, j)
                    og = gst_sl(2, j)
                    cc = gst_sl(3, j)
                    cT_j = _c(cT_o + 256 * j, 256)
                    if st >= 1:
                        v.tensor_tensor(ctmp, fg, cT_j, ALU.mult)
                        v.tensor_tensor(cT_j, ig, cc, ALU.mult).then_inc(sems["vt"], 1)
                        vtn[0] += 1
                        SW(v, "vt", vtn[0])
                        v.tensor_tensor(cT_j, cT_j, ctmp, ALU.add).then_inc(sems["cupd"], 1)
                    else:
                        v.tensor_tensor(cT_j, ig, cc, ALU.mult).then_inc(sems["cupd"], 1)
                    SW(v, "tanh", 8 * st + j + 1)
                    v.tensor_tensor(hT(st % 2)[:, 256 * j:256 * (j + 1)], og,
                                    _b(ctt_o + 256 * (j % 2), 256), ALU.mult
                                    ).then_inc(sems["cv"], 1)
                    if j >= 3:
                        hs_v(st, j - 3)
                    if st >= 1 and j == 2:
                        u = st - 1
                        SW(v, "tphs", 16 * u + 8)
                        v.tensor_tensor(_c(xin_o[u % 3], 1024),
                                        _c(xin_o[u % 3], 1024),
                                        pt[:, 1024:2048], ALU.add).then_inc(sems["xn"], 1)
                v.engine_nop().then_inc(sems["vt"], 1)
                vtn[0] += 1
                SW(v, "vt", vtn[0])
                for jj in (5, 6, 7):
                    hs_v(st, jj)

            u = S - 1
            for h in (0, 1):
                SW(v, "tphs", 16 * u + 8 * (h + 1))
                v.tensor_tensor(_c(xin_o[u % 3] + 1024 * h, 1024),
                                _c(xin_o[u % 3] + 1024 * h, 1024),
                                pt[:, 1024:2048], ALU.add).then_inc(sems["xn"], 1)

            def eb_chain(par, dref, outsem):
                v.tensor_tensor(fs(F_CM, par), fs(F_U, par), fs(F_TF, par),
                                ALU.is_lt).then_inc(sems["vt"], 1)
                vtn[0] += 1
                SW(v, "vt", vtn[0])
                v.tensor_tensor(fs(F_EB, par), fs(F_TF, par), fs(F_CM, par),
                                ALU.subtract).then_inc(sems["vt"], 1)
                vtn[0] += 1
                v.tensor_scalar(fs(F_CM, par), fs(F_AMAX, par), 1e-38, None,
                                ALU.is_ge).then_inc(sems["vt"], 1)
                vtn[0] += 1
                SW(v, "vt", vtn[0])
                v.tensor_scalar(fs(F_EB, par), fs(F_EB, par), -160.0, None,
                                ALU.add).then_inc(sems["vt"], 1)
                vtn[0] += 1
                SW(v, "vt", vtn[0])
                v.tensor_tensor(fs(F_EB, par), fs(F_EB, par), fs(F_CM, par),
                                ALU.mult).then_inc(sems["vt"], 1)
                vtn[0] += 1
                SW(v, "vt", vtn[0])
                if dref[0] == "s":
                    v.tensor_scalar(fs(F_EB, par), fs(F_EB, par), dref[1], None,
                                    ALU.add).then_inc(sems[outsem], 1)
                else:
                    v.tensor_tensor(fs(F_EB, par), fs(F_EB, par), dref[1],
                                    ALU.add).then_inc(sems[outsem], 1)

            def staircase(src, xq_ap, tA, tB, tC, tD, par):
                v.tensor_tensor(v3(tA), v3(src), rep32(fs(F_RS, par)), ALU.mult)
                v.scalar_tensor_tensor(tB, tA, -1.0, tA, ALU.mult, ALU.max)
                v.tensor_scalar(tC, tB, 0.25, None, ALU.is_ge)
                v.tensor_scalar(tD, tB, 0.75, None, ALU.is_ge)
                v.tensor_add(tC, tC, tD)
                v.tensor_scalar(tD, tB, 1.25, None, ALU.is_ge)
                v.tensor_add(tC, tC, tD)
                v.tensor_scalar(tD, tB, 1.75, None, ALU.is_ge)
                v.tensor_add(tC, tC, tD)
                v.tensor_scalar(tD, tB, 2.5, None, ALU.is_ge)
                v.tensor_scalar(tB, tB, 3.5, None, ALU.is_ge)
                v.tensor_add(tD, tD, tB)
                v.scalar_tensor_tensor(tC, tC, 0.5, tD, ALU.mult, ALU.add)
                v.scalar_tensor_tensor(tD, tA, -1.0, tA, ALU.mult, ALU.max)
                v.tensor_scalar(tD, tD, 5.0, None, ALU.is_ge)
                v.scalar_tensor_tensor(tC, tD, 2.0, tC, ALU.mult, ALU.add)
                v.tensor_scalar(tD, tA, 0.0, 2.0, ALU.is_ge, ALU.mult)
                v.scalar_tensor_tensor(tD, tD, -1.0, tC, ALU.add, ALU.mult)
                v.tensor_tensor(v3(xq_ap), v3(tD), rep32(scbp(par)), ALU.mult)

            def blockchain_head(src, par):
                v.tensor_reduce(
                    fs(F_AMAX, par).rearrange("p (b o) -> p b o", o=1),
                    v3(src), AXX, ALU.max, apply_absolute_value=True).then_inc(sems["vt"], 1)
                vtn[0] += 1
                SW(v, "vt", vtn[0])
                v.tensor_scalar(fs(F_AMX, par), fs(F_AMAX, par), 1e-30, None, ALU.max)

            wq_pending = []

            for oi in range(8):
                par = oi % 2
                if oi >= 1:
                    wq_pending.append(oi - 1)
                while wq_pending:
                    _wqe_oi = wq_pending.pop(0)
                    SW(v, "wqt", 8 * (_wqe_oi + 1))
                    for j in range(8):
                        v.tensor_copy(_b(wqT_o + 1024 * j + 128 * _wqe_oi, 128),
                                      pt[:, 1024 + 128 * j:1024 + 128 * (j + 1)]
                                      ).then_inc(sems["wqe"], 1)
                SW(v, "wqd%d" % par, 16 * (oi // 2 + 1))
                if oi >= 2:
                    SW(v, "wql", oi - 1)
                wf_ap = _b(wf_o[par], 1024)
                blockchain_head(wf_ap, par)
                v.engine_nop().then_inc(sems["wqa"], 1)
                SW(v, "wql", oi + 1)
                v.tensor_scalar(fs(F_U, par), fs(F_L, par), 1.0 / LN2C, 158.0,
                                ALU.mult, ALU.add).then_inc(sems["wqu"], 1)
                SW(v, "wqi", oi + 1)
                eb_chain(par, ("t", dwh[:, 32 * oi:32 * (oi + 1)]), "wqx")
                SW(v, "wqc", oi + 1)
                SW(v, "wqt", 8 * oi)
                staircase(wf_ap, _b(xq_o[0], 1024), _b(tA_o, 1024), _b(tB_o, 1024),
                          _b(tC_o, 1024), _b(tD2_o, 1024), par)
                v.engine_nop().then_inc(sems["wq"], 1)

            def wq_evacs(oi):
                SW(v, "wqt", 8 * (oi + 1))
                for j in range(8):
                    v.tensor_copy(_b(wqT_o + 1024 * j + 128 * oi, 128),
                                  pt[:, 1024 + 128 * j:1024 + 128 * (j + 1)]
                                  ).then_inc(sems["wqe"], 1)

            wq_evacs(7)

            for k in range(2 * S + 2):
                if 1 <= k <= 2 * S:
                    j = k - 1
                    par = j % 2
                    SW(v, "fsc", j + 1)
                    SW(v, "ftp", 8 * (j - 1))
                    if j == 0:
                        SW(v, "wqt", 64)
                    staircase(_c(h2_o, 1024), _b(xq_o[par], 1024), _b(tA_o, 1024),
                              _b(tB_o, 1024), _b(tC_o, 1024), _b(tD_o, 1024), par)
                    v.engine_nop().then_inc(sems["fq"], 1)
                if k >= 2:
                    j = k - 2
                    par = j % 2
                    SW(v, "fbe", 3 * (j // 2) + 3)
                    SW(v, "fmm", 2 * (j + 1))
                    gf = _c(h2_o, 1024)
                    v.tensor_tensor(gf, pf[:, 1024 * par:1024 * (par + 1)],
                                    _c(bgb_o, 1024), ALU.mult)
                    v.tensor_tensor(_c(xr4_o[j % 4], 1024), gf, _c(xr4_o[j % 4], 1024),
                                    ALU.add).then_inc(sems["fout"], 1)
                if k <= 2 * S - 1:
                    s_, h_, par = k // 2, k % 2, k % 2
                    SW(v, "fno", k + 1)
                    SW(v, "fbe", 2 if s_ == 0 else 3 * s_ + 2)
                    h2 = _c(h2_o, 1024)
                    v.tensor_tensor(h2, _c(tln_o, 1024), cbS, ALU.mult)
                    v.tensor_tensor(h2, h2, sbS, ALU.add).then_inc(sems["fmo"], 1)
                    if k >= 2:
                        SW(v, "fln", k - 1)
                    else:
                        SW(v, "wql", 8)
                    blockchain_head(h2, par)
                    v.engine_nop().then_inc(sems["fam"], 1)
                    SW(v, "fln", k + 1)
                    v.tensor_scalar(fs(F_U, par), fs(F_L, par), 1.0 / LN2C, 158.0,
                                    ALU.mult, ALU.add).then_inc(sems["fu"], 1)
                    SW(v, "fif", k + 1)
                    eb_chain(par, ("s", dhalf[:, :]), "fex")

        @blk.sync
        def _(sync):
            for uu in (0, 1, 2):
                for h in (0, 1):
                    sync.dma_start(out=_c(xin_o[uu % 3] + 1024 * h, 1024),
                                   in_=x_d[uu, 128 * h:128 * (h + 1), :]
                                   ).then_inc(sems["xi%d%d" % (uu % 3, h)], 16)
            for st in range(S):
                if st >= 1:
                    u = st - 1
                    for h in (0, 1):
                        SW(sync, "xn", 2 * u + h + 1)
                        sync.dma_start(out=xnew_d[u, 128 * h:128 * (h + 1), :],
                                       in_=_c(xin_o[u % 3] + 1024 * h, 1024)
                                       ).then_inc(sems["xst%d" % (u % 2)], 16)
                uu = st + 2
                if 3 <= uu <= S - 1:
                    SW(sync, "xst%d" % ((uu - 3) % 2), 32 * ((uu - 3) // 2 + 1))
                    for h in (0, 1):
                        sync.dma_start(out=_c(xin_o[uu % 3] + 1024 * h, 1024),
                                       in_=x_d[uu, 128 * h:128 * (h + 1), :]
                                       ).then_inc(sems["xi%d%d" % (uu % 3, h)], 16)
            u = S - 1
            for h in (0, 1):
                SW(sync, "xn", 2 * u + h + 1)
                sync.dma_start(out=xnew_d[u, 128 * h:128 * (h + 1), :],
                               in_=_c(xin_o[u % 3] + 1024 * h, 1024)
                               ).then_inc(sems["xst%d" % (u % 2)], 16)

            SW(sync, "xst0", 32 * 8)
            SW(sync, "xst1", 32 * 8)
            for i in range(4):
                sync.dma_start(out=_c(xr4_o[i % 4], 1024),
                               in_=xnew_d[i // 2, 128 * (i % 2):128 * (i % 2 + 1), :]
                               ).then_inc(sems["rb%d" % (i % 2)], 16)
            for i in range(2 * S):
                s_, h_ = i // 2, i % 2
                SW(sync, "fout", i + 1)
                sync.dma_start(out=out_d[s_, 128 * h_:128 * (h_ + 1), :],
                               in_=_c(xr4_o[i % 4], 1024)).then_inc(sems["ost%d" % (i % 2)], 16)
                if i + 4 < 2 * S:
                    jj = i + 4
                    SW(sync, "ost%d" % (i % 2), 16 * (i // 2 + 1))
                    sync.dma_start(out=_c(xr4_o[jj % 4], 1024),
                                   in_=xnew_d[jj // 2, 128 * (jj % 2):128 * (jj % 2 + 1), :]
                                   ).then_inc(sems["rb%d" % (jj % 2)], 16)
            sync.wait_ge(sems["ost0"], 16 * S)
            sync.wait_ge(sems["ost1"], 16 * S)
    return nc


def kernel(L0_attn_input_seq, L0_attn_w, L0_attn_u, L0_attn_bias,
           L0_ffn_weight, L0_ffn_bias, L0_ffn_delta_w, L0_ffn_delta_act,
           L0_c, L0_wmod):
    x = np.ascontiguousarray(np.asarray(L0_attn_input_seq, dtype=np.float32))
    w = np.ascontiguousarray(np.asarray(L0_attn_w, dtype=np.float32))
    u = np.ascontiguousarray(np.asarray(L0_attn_u, dtype=np.float32))
    bias = np.ascontiguousarray(np.asarray(L0_attn_bias, dtype=np.float32))
    wffn = np.ascontiguousarray(np.asarray(L0_ffn_weight, dtype=np.float32))
    bffn = np.ascontiguousarray(np.asarray(L0_ffn_bias, dtype=np.float32)).reshape(1, -1)
    dw = np.ascontiguousarray(np.asarray(L0_ffn_delta_w, dtype=np.float32))
    dact = np.ascontiguousarray(np.asarray(L0_ffn_delta_act, dtype=np.float32)).reshape(1, 1)
    cmod = np.ascontiguousarray(np.asarray(L0_c, dtype=np.float32))
    wmod = np.ascontiguousarray(np.asarray(L0_wmod, dtype=np.float32))

    identb = np.eye(128, dtype=ml_dtypes.bfloat16)
    identf = np.eye(128, dtype=np.float32)
    ones1 = np.ones((1, 128), dtype=ml_dtypes.bfloat16)

    nc = build_program()
    in_maps = []
    for c in range(NCORES):
        xs = np.ascontiguousarray(x[:, B * c:B * (c + 1), :])
        in_maps.append(dict(x=xs, w_attn=w, u_attn=u, bias_attn=bias,
                            wmod=wmod, cmod=cmod, wffn=wffn, bffn=bffn,
                            dw=dw, dact=dact, identb=identb, identf=identf,
                            ones1=ones1))
    res = run_bass_kernel_spmd(nc, in_maps, core_ids=list(range(NCORES)))
    return np.concatenate([r["out"] for r in res.results], axis=1)
